# version 09
# baseline (speedup 1.0000x reference)
import numpy as np
import ml_dtypes
import contextlib
import concourse.bass as bass
import concourse.mybir as mybir
from concourse.bass_utils import run_bass_kernel_spmd

F32 = mybir.dt.float32
BF16 = mybir.dt.bfloat16
F16 = mybir.dt.float16
AF = mybir.ActivationFunctionType
ALU = mybir.AluOpType
AX = mybir.AxisListType

B, S, D = 8, 2048, 1024
E, TOPK, H = 64, 8, 256
ES, HS = 2, 2048
L = 64
NT = S // 128
NDK = D // 128
NCH = 4
G = 4
RING = 6
SLOT = 4096
HSW_OFF = RING * SLOT
EPS = 1e-5

ENGINES = ("sync", "gpsimd", "tensor", "vector", "scalar")


class Ctr:
    def __init__(self, nc, stack, name, prog=None, eng=None):
        self.h = stack.enter_context(nc.semaphore(name))
        self.n = 0
        self.prog = prog
        self.eng = eng

    def take(self, k=1, auto=True):
        prev = self.n
        self.n += k
        if auto and self.prog is not None and self.eng is not None:
            if self.prog.pending[self.eng] is None:
                self.prog.pending[self.eng] = (self.h, prev)
        return self.n


class Prog:
    def __init__(self):
        self.q = {e: [] for e in ENGINES}
        self.pending = {e: None for e in ENGINES}

    def on(self, eng, fn):
        p = self.pending[eng]
        if p is not None:
            self.pending[eng] = None
            if p[1] > 0:
                self.q[eng].append(lambda e, h=p[0], v=p[1]: e.wait_ge(h, v))
        self.q[eng].append(fn)

    def wait(self, eng, ctr, val=None):
        v = ctr.n if val is None else val
        assert v <= ctr.n, "waiting on a future milestone"
        if v > 0:
            self.q[eng].append(lambda e, h=ctr.h, v=v: e.wait_ge(h, v))


def build_nc(phases=None, dbg=False):
    if phases is None:
        phases = ["ln1", "rt1", "moe1", "sh1", "ln2", "attn", "ln3", "rt2", "moe2", "sh2"]
    nc = bass.Bass("TRN2", target_bir_lowering=False, debug=False)
    dr = {}

    def din(name, shape, dt=F32):
        dr[name] = nc.dram_tensor(name, list(shape), dt, kind="ExternalInput").ap()
        return dr[name]

    x_in = din("x", (S, D))
    w_q = din("wq", (D, D))
    w_k = din("wk", (D, D))
    w_v = din("wv", (D, D))
    b_q = din("bq", (1, D))
    b_k = din("bk", (1, D))
    b_v = din("bv", (1, D))
    w_r = din("router_w", (D, E))
    b_r = din("router_b", (1, E))
    w1 = din("w1", (E, D, H))
    b1t = din("b1t", (H, E))
    w2 = din("w2", (E, H, D))
    b2 = din("b2", (E, D))
    sw1 = din("sw1", (ES, D, HS))
    sb1t = din("sb1t", (HS, ES))
    sw2 = din("sw2", (ES, HS, D))
    sb2 = din("sb2", (ES, D))
    c_idbf = din("c_idbf", (128, 128), BF16)
    c_idf16 = din("c_idf16", (128, 128), F16)
    c_idf32 = din("c_idf32", (128, 128), F32)
    c_id64 = din("c_id64", (64, 64), F16)
    c_ones1 = din("c_ones1", (1, 128), F16)
    c_ones2 = din("c_ones2", (2, 128), F16)
    c_apat = din("c_apat", (128, NT * L), F16)
    x_out = nc.dram_tensor("out", [S, D], F32, kind="ExternalOutput").ap()
    if dbg:
        dbg_uT = nc.dram_tensor("dbg_uT", [128, NDK * 2048], BF16, kind="ExternalOutput").ap()
        dbg_cw = nc.dram_tensor("dbg_cw", [128, NT * 64], F32, kind="ExternalOutput").ap()
        dbg_cwT = nc.dram_tensor("dbg_cwT", [64, S], BF16, kind="ExternalOutput").ap()
        dbg_klT = nc.dram_tensor("dbg_klT", [128, NDK * L], F16, kind="ExternalOutput").ap()
        dbg_mid = nc.dram_tensor("dbg_mid", [64, D], F16, kind="ExternalOutput").ap()
        dbg_kn = nc.dram_tensor("dbg_kn", [128, 16384], F16, kind="ExternalOutput").ap()

    stack = contextlib.ExitStack()

    def sb(name, shape, dt):
        return stack.enter_context(nc.sbuf_tensor(name, list(shape), dt))

    x_sb = sb("x_sb", (128, NT * 1024), F32)
    uT = sb("uT", (128, NDK * 2048), BF16)
    arena = sb("arena", (128, 32768), BF16)
    hs_tmp = sb("hs_tmp", (128, 2 * 512), F32)
    scr = sb("scr", (128, 2 * 1024), F32)
    stg = sb("stg", (128, 4 * 1024), BF16)
    cw05 = sb("cw05", (128, NT * 64), F32)
    cwT = sb("cwT", (128, S), BF16)
    sm = sb("sm", (128, 64), F32)
    smeps = sb("smeps", (128, 1), F32)
    smdum = sb("smdum", (128, 4), F32)
    klT = sb("klT", (128, NDK * L), F16)
    midb = sb("midb", (64, D), F16)
    k1s = sb("k1s", (64, 256), F16)
    k2s = sb("k2s", (128, 128), F16)
    b1T_sb = sb("b1T_sb", (128, 2 * 64), F32)
    sb1T_sb = sb("sb1T_sb", (128, 16 * 2), F32)
    b2_sb = sb("b2_sb", (128, D), F16)
    wr_sb = sb("wr_sb", (128, NDK * E), BF16)
    rb_sb = sb("rb_sb", (1, E), F16)
    bq_sb = sb("bq_sb", (1, D), F16)
    bk_sb = sb("bk_sb", (1, D), F16)
    bv_sb = sb("bv_sb", (1, D), F16)
    id_bf = sb("id_bf", (128, 128), BF16)
    id_f16 = sb("id_f16", (128, 128), F16)
    id_f32 = sb("id_f32", (128, 128), F32)
    id64 = sb("id64", (64, 64), F16)
    ones1 = sb("ones1", (1, 128), F16)
    apat = sb("apat", (128, NT * L), F16)

    def av(off, n):
        return arena[:, off:off + n]

    def av16(off, n):
        return arena[:, off:off + n].bitcast(F16)

    pA = stack.enter_context(nc.psum_tensor("pA", [128, 1024], F32))
    pB = stack.enter_context(nc.psum_tensor("pB", [128, 1024], F32))
    pF = stack.enter_context(nc.psum_tensor("pF", [128, 2048], F32))

    P = Prog()
    HMS = {}
    ctr = {e: Ctr(nc, stack, f"c_{e}", prog=P, eng=e) for e in ENGINES}
    tc, vc, scc = ctr["tensor"], ctr["vector"], ctr["scalar"]
    dring = [Ctr(nc, stack, f"dr{i}") for i in range(RING)]
    dring2 = [Ctr(nc, stack, f"dr2_{i}") for i in range(RING)]
    dsw = [Ctr(nc, stack, f"dsw{i}") for i in range(4)]
    dx = Ctr(nc, stack, "dx")
    dcst = Ctr(nc, stack, "dc")
    dcstg = Ctr(nc, stack, "dcg")
    dout = Ctr(nc, stack, "do")

    def join():
        snaps = {}
        for name in ENGINES:
            c = ctr[name]
            v = c.take()
            if name == "scalar":
                P.on(name, lambda e, h=c.h: e.activation(
                    out=smdum[:, 3:4], in_=smeps[:, 0:1], func=AF.Identity,
                    bias=0.0, scale=1.0).then_inc(h, 1))
            elif name == "vector":
                P.on(name, lambda e, h=c.h: e.tensor_copy(
                    out=smdum[:, 1:2], in_=smdum[:, 2:3]).then_inc(h, 1))
            else:
                P.on(name, lambda e, h=c.h: e.nop().then_inc(h, 1))
            snaps[name] = v
        for name in ENGINES:
            for other in ENGINES:
                if other != name:
                    P.wait(name, ctr[other], snaps[other])

    def ph_load():
        xs = x_in.rearrange("(tt p) d -> p tt d", p=128)
        nx = dx.take(16 * 4)

        def f(e):
            for i in range(4):
                if i > 0:
                    e.wait_ge(dx.h, 16 * i)
                dst = x_sb[:, i * 4096:(i + 1) * 4096].rearrange("p (tt d) -> p tt d", d=1024)
                e.dma_start(out=dst, in_=xs[:, i * 4:(i + 1) * 4, :]).then_inc(dx.h, 16)
            j = 0
            for dst, src in ((id_bf, c_idbf), (id_f16, c_idf16), (id_f32, c_idf32),
                             (id64, c_id64), (ones1, c_ones1),
                             (apat, c_apat)):
                if j > 0:
                    e.wait_ge(dcst.h, 16 * j)
                e.dma_start(out=dst[:, :], in_=src).then_inc(dcst.h, 16)
                j += 1
        dcst.take(16 * 6)
        P.on("sync", f)

        def g(e):
            srcs = [None] * 9
            e.dma_start(out=wr_sb[:, :].rearrange("p (dk e2) -> p dk e2", e2=E),
                        in_=w_r.rearrange("(dk p) e2 -> p dk e2", p=128)).then_inc(dcstg.h, 16)
            j = 1
            for dst, src in ((rb_sb, b_r), (bq_sb, b_q), (bk_sb, b_k), (bv_sb, b_v)):
                e.wait_ge(dcstg.h, 16 * j)
                e.dma_start(out=dst[:, :], in_=src).then_inc(dcstg.h, 16)
                j += 1
            e.wait_ge(vc.h, 1)
            e.wait_ge(dcstg.h, 16 * j); j += 1
            e.dma_start(out=b2_sb[0:64, :], in_=b2).then_inc(dcstg.h, 16)
            e.wait_ge(dcstg.h, 16 * j); j += 1
            e.dma_start(out=b2_sb[64:66, :], in_=sb2).then_inc(dcstg.h, 16)
            e.wait_ge(dcstg.h, 16 * j); j += 1
            e.dma_start(out=b1T_sb[:, :].rearrange("p (hk e2) -> p hk e2", e2=64),
                        in_=b1t.rearrange("(hk p) e2 -> p hk e2", p=128)).then_inc(dcstg.h, 16)
            e.wait_ge(dcstg.h, 16 * j); j += 1
            e.dma_start(out=sb1T_sb[:, :].rearrange("p (hk e2) -> p hk e2", e2=2),
                        in_=sb1t.rearrange("(hk p) e2 -> p hk e2", p=128)).then_inc(dcstg.h, 16)
        dcstg.take(16 * 9)
        P.on("gpsimd", g)
        v1_ = vc.take()
        v2_ = vc.take()

        def fmemset(e):
            e.memset(smeps[:, :], EPS)
            e.memset(smdum[:, :], 0.0)
            e.memset(sm[:, :], 0.0)
            e.memset(b2_sb[:, :], 0.0)
            e.memset(cwT[:, :], 0.0).then_inc(vc.h, 1)
            e.wait_ge(vc.h, v1_)
            e.memset(cwT[64:66, :], 1.0).then_inc(vc.h, 1)
        P.on("vector", fmemset)
        for name in ("tensor", "vector", "scalar"):
            P.wait(name, vc)
        P.wait("tensor", dcst, 16)
        HMS["dx"] = dx
        HMS["dcst_all"] = dcst.n
        HMS["dcstg_all"] = dcstg.n

    def emit_ln():
        sqs = [0] * NT
        dxh = HMS.get("dx")
        for tt in range(NT):
            xt = x_sb[:, tt * 1024:(tt + 1) * 1024]
            if dxh is not None and tt % 4 == 0:
                P.wait("vector", dxh, 16 * (tt // 4 + 1))
                P.wait("scalar", dxh, 16 * (tt // 4 + 1))
            vc.take()
            P.on("vector", lambda e, xt=xt, tt=tt: e.tensor_reduce(
                out=sm[:, tt:tt + 1], in_=xt, axis=AX.X,
                op=ALU.add).then_inc(vc.h, 1))
            if tt >= 2:
                P.wait("scalar", scc, sqs[tt - 2])
            last = (tt == NT - 1)
            pv = scc.take()
            sq = scc.take() if last else pv
            sqs[tt] = sq

            def fsq(e, xt=xt, tt=tt, last=last, pv=pv):
                e.activation(
                    out=scr[:, (tt % 2) * 1024:(tt % 2 + 1) * 1024], in_=xt,
                    func=AF.Square, bias=0.0, scale=1.0,
                    accum_out=sm[:, 16 + tt:17 + tt]).then_inc(scc.h, 1)
                if last:
                    e.wait_ge(scc.h, pv)
                    e.activation(out=smdum[:, 0:1], in_=sm[:, 16 + tt:17 + tt],
                                 func=AF.Identity, bias=0.0, scale=1.0).then_inc(scc.h, 1)
            P.on("scalar", fsq)
        v0 = vc.n
        P.wait("vector", vc, v0)
        P.wait("vector", scc)
        v1 = vc.take()

        def stats(e):
            e.tensor_scalar(out=sm[:, 32:48], in0=sm[:, 0:16], scalar1=1.0 / D,
                            scalar2=None, op0=ALU.mult)
            e.tensor_scalar(out=sm[:, 16:32], in0=sm[:, 16:32], scalar1=1.0 / D,
                            scalar2=None, op0=ALU.mult).then_inc(vc.h, 1)
        P.on("vector", stats)
        P.wait("vector", vc, v1)
        v2 = vc.take()
        P.on("vector", lambda e: e.tensor_tensor(
            out=sm[:, 48:64], in0=sm[:, 32:48], in1=sm[:, 32:48],
            op=ALU.mult).then_inc(vc.h, 1))
        P.wait("vector", vc, v2)
        v3 = vc.take()
        P.on("vector", lambda e: e.tensor_tensor(
            out=sm[:, 48:64], in0=sm[:, 16:32], in1=sm[:, 48:64],
            op=ALU.subtract).then_inc(vc.h, 1))
        P.wait("scalar", vc, v3)
        s0 = scc.take()
        s1 = scc.take()

        def fsqrt(e, s0=s0):
            e.activation(out=sm[:, 48:64], in_=sm[:, 48:64], func=AF.Sqrt,
                         bias=smeps[:, 0:1], scale=1.0).then_inc(scc.h, 1)
            e.wait_ge(scc.h, s0)
            e.activation(out=smdum[:, 0:1], in_=sm[:, 63:64], func=AF.Identity,
                         bias=0.0, scale=1.0).then_inc(scc.h, 1)
        P.on("scalar", fsqrt)
        P.wait("vector", scc, s1)
        v4 = vc.take()
        P.on("vector", lambda e: e.reciprocal(
            out=sm[:, 48:64], in_=sm[:, 48:64]).then_inc(vc.h, 1))
        P.wait("vector", vc, v4)

        tdone = [0] * NT
        edone = [0] * NT
        for tt in range(NT):
            xt = x_sb[:, tt * 1024:(tt + 1) * 1024]
            ub = stg[:, (tt % 2) * 1024:(tt % 2 + 1) * 1024]
            if tt >= 2:
                P.wait("vector", tc, tdone[tt - 2])
            va = vc.take()

            def fapply(e, xt=xt, ub=ub, tt=tt):
                e.tensor_scalar(out=ub, in0=xt, scalar1=sm[:, 32 + tt:33 + tt],
                                scalar2=sm[:, 48 + tt:49 + tt], op0=ALU.subtract,
                                op1=ALU.mult).then_inc(vc.h, 1)
            P.on("vector", fapply)
            ptb = pB[:, (tt % 2) * 512:(tt % 2 + 1) * 512].bitcast(BF16)
            P.wait("tensor", vc, va)
            if tt >= 2:
                P.wait("tensor", scc, edone[tt - 2])
            prevt = tc.n
            td = tc.take(auto=False)

            def ftr(e, ub=ub, ptb=ptb, prevt=prevt):
                for dk in range(NDK):
                    if dk == NDK - 1 and prevt > 0:
                        e.wait_ge(tc.h, prevt)
                    ins = e.transpose(out=ptb[:, dk * 128:(dk + 1) * 128],
                                      in_=ub[:, dk * 128:(dk + 1) * 128],
                                      identity=id_bf[:, :])
                ins.then_inc(tc.h, 1)
            P.on("tensor", ftr)
            tdone[tt] = td
            P.wait("scalar", tc, td)
            pv0 = scc.take()
            sd = scc.take()

            def fev(e, ptb=ptb, tt=tt, pv0=pv0):
                d3 = uT[:, :].rearrange("p (dk t) -> p dk t", t=2048)[:, :, tt * 128:(tt + 1) * 128]
                e.copy(out=d3,
                       in_=ptb.rearrange("p (dk t) -> p dk t", t=128)).then_inc(scc.h, 1)
                e.wait_ge(scc.h, pv0)
                e.activation(out=smdum[:, 0:1], in_=uT[:, tt * 128:tt * 128 + 1],
                             func=AF.Identity, bias=0.0, scale=1.0).then_inc(scc.h, 1)
            P.on("scalar", fev)
            edone[tt] = sd

    def emit_router(level=3):
        if "dcstg_all" in HMS:
            P.wait("tensor", dcstg, HMS["dcstg_all"])
            P.wait("tensor", dcst, HMS["dcst_all"])
            P.wait("scalar", dcstg, HMS["dcstg_all"])
            P.wait("vector", dcstg, HMS["dcstg_all"])
        exp_done = [0] * NT
        tr_ev = [0] * NT
        for tt in range(NT):
            par = tt % 2
            pl = pF[:, par * 512:par * 512 + 64]
            if tt >= 2:
                P.wait("tensor", scc, exp_done[tt - 2])
            prevt = tc.n
            td = tc.take(auto=False)

            def fmm(e, pl=pl, tt=tt, prevt=prevt):
                for dk in range(NDK):
                    e.matmul(out=pl,
                             lhsT=uT[:, dk * 2048 + tt * 128:dk * 2048 + (tt + 1) * 128],
                             rhs=wr_sb[:, dk * 64:(dk + 1) * 64], start=(dk == 0),
                             stop=False, skip_group_check=True)
                if prevt > 0:
                    e.wait_ge(tc.h, prevt)
                e.matmul(out=pl, lhsT=ones1[:, :], rhs=rb_sb[:, :], start=False,
                         stop=True, skip_group_check=True).then_inc(tc.h, 1)
            P.on("tensor", fmm)
            P.wait("vector", tc, td)
            if tt >= 2:
                P.wait("vector", scc, exp_done[tt - 2])
            vm = vc.take()
            P.on("vector", lambda e, pl=pl, par=par: e.reduce_max(
                out=sm[:, par:par + 1], in_=pl, axis=AX.X,
                negate=True).then_inc(vc.h, 1))
            P.wait("scalar", vc, vm)
            pv0 = scc.take()
            sd = scc.take()

            def fexp(e, pl=pl, tt=tt, par=par, pv0=pv0):
                e.activation(out=scr[:, tt * 64:(tt + 1) * 64], in_=pl, func=AF.Exp,
                             bias=sm[:, par:par + 1], scale=1.0).then_inc(scc.h, 1)
                e.wait_ge(scc.h, pv0)
                e.activation(out=smdum[:, 0:1], in_=scr[:, tt * 64 + 63:tt * 64 + 64],
                             func=AF.Identity, bias=0.0, scale=1.0).then_inc(scc.h, 1)
            P.on("scalar", fexp)
            exp_done[tt] = sd

        E3 = scr[:, 0:1024].rearrange("p (t v) -> p t v", v=64)
        C = scr[:, 1024:2048]
        C3 = C.rearrange("p (t v) -> p t v", v=64)
        M = hs_tmp[:, 0:1024]
        M3 = M.rearrange("p (t v) -> p t v", v=64)
        mx = sm[:, 16:32]
        mxb = mx.unsqueeze(2).broadcast_to((128, 16, 64))
        P.wait("vector", scc)
        vals = [vc.take() for _ in range(3 * (TOPK - 1) + 8)]

        def ftopk(e, vals=tuple(vals)):
            i = [0]

            def step(ins):
                ins.then_inc(vc.h, 1)
                e.wait_ge(vc.h, vals[i[0]])
                i[0] += 1
            step(e.tensor_copy(out=C, in_=scr[:, 0:1024]))
            for _ in range(TOPK - 1):
                step(e.tensor_reduce(out=mx, in_=C3, axis=AX.X, op=ALU.max))
                step(e.tensor_tensor(out=M3, in0=C3, in1=mxb, op=ALU.is_ge))
                step(e.scalar_tensor_tensor(out=C, in0=M, scalar=-1e30, in1=C,
                                            op0=ALU.mult, op1=ALU.add))
            step(e.tensor_reduce(out=mx, in_=C3, axis=AX.X, op=ALU.max))
            step(e.tensor_tensor(out=M3, in0=E3, in1=mxb, op=ALU.is_ge))
            step(e.tensor_tensor(out=C, in0=M, in1=scr[:, 0:1024], op=ALU.mult))
            step(e.tensor_reduce(out=mx, in_=C3, axis=AX.X, op=ALU.add))
            step(e.reciprocal(out=mx, in_=mx))
            step(e.tensor_tensor(out=cw05[:, :].rearrange("p (t v) -> p t v", v=64),
                                 in0=C3, in1=mxb, op=ALU.mult))
            step(e.tensor_copy(out=smdum[:, 1:2], in_=smdum[:, 2:3]))
            assert i[0] == len(vals)
        P.on("vector", ftopk)
        cw_done = vals[-1]

        trc_done = [0] * NT
        for tt in range(NT):
            par = tt % 2
            cw16 = stg[:, par * 64:par * 64 + 64].bitcast(F16)
            P.wait("vector", vc, cw_done)
            if tt >= 2:
                P.wait("vector", tc, trc_done[tt - 2])
            vcast = vc.take()

            def fcwcast(e, tt=tt, cw16=cw16):
                e.tensor_copy(out=cw16,
                              in_=cw05[:, tt * 64:(tt + 1) * 64]).then_inc(vc.h, 1)
            P.on("vector", fcwcast)
            ptr16 = pF[:, 1024 + par * 512:1024 + par * 512 + 64].bitcast(F16)
            P.wait("tensor", vc, vcast)
            if tt >= 2:
                P.wait("tensor", scc, tr_ev[tt - 2])
            prevt = tc.n
            td2 = tc.take(auto=False)

            def ftrc(e, ptr16=ptr16, cw16=cw16, prevt=prevt):
                if prevt > 0:
                    e.wait_ge(tc.h, prevt)
                e.transpose(out=ptr16[0:64, :], in_=cw16,
                            identity=id_f16[:, :]).then_inc(tc.h, 1)
            P.on("tensor", ftrc)
            trc_done[tt] = td2
            P.wait("scalar", tc, td2)
            pv0 = scc.take()
            se = scc.take()

            def fcwev(e, ptr16=ptr16, tt=tt, pv0=pv0):
                e.copy(out=cwT[0:64, tt * 128:(tt + 1) * 128],
                       in_=ptr16[0:64, :]).then_inc(scc.h, 1)
                e.wait_ge(scc.h, pv0)
                e.activation(out=smdum[0:64, 0:1], in_=cwT[0:64, tt * 128:tt * 128 + 1],
                             func=AF.Identity, bias=0.0, scale=1.0).then_inc(scc.h, 1)
            P.on("scalar", fcwev)
            tr_ev[tt] = se
        HMS["exp_done"] = list(exp_done)
        HMS["tr_ev"] = list(tr_ev)

    def emit_routed(mlevel=4):
        rt_exp = HMS.get("exp_done", [0] * NT)
        rt_tr = HMS.get("tr_ev", [0] * NT)
        slot_free_w1 = [0] * RING
        slot_free_w2 = [0] * RING
        slot_loaded = [None] * RING
        hsw_mm2_done = [0, 0]
        silu_free = [0, 0]
        mm1_bank_free = [0, 0]
        cw_bank_free = [0, 0]
        f_bank_free = [0, 0, 0, 0]

        def load_expert(eidx):
            s = eidx % RING
            need1 = slot_free_w1[s]
            need2 = slot_free_w2[s]
            p1 = dring[s].n
            n1 = dring[s].take(16)
            p2 = dring2[s].n
            n2 = dring2[s].take(16)

            def f(e, eidx=eidx, s=s, need1=need1, need2=need2, p1=p1, p2=p2):
                if need1 > 0:
                    e.wait_ge(tc.h, need1)
                if p1 > 0:
                    e.wait_ge(dring[s].h, p1)
                src1 = w1[eidx:eidx + 1].rearrange("e2 (dk p) h -> p (e2 dk) h", p=128)
                e.dma_start(out=av(s * SLOT, 2048).rearrange("p (dk h) -> p dk h", h=H),
                            in_=src1).then_inc(dring[s].h, 16)
                if need2 > 0:
                    e.wait_ge(tc.h, need2)
                if p2 > 0:
                    e.wait_ge(dring2[s].h, p2)
                src2 = w2[eidx:eidx + 1].rearrange("e2 (hk p) d -> p (e2 hk) d", p=128)
                e.dma_start(out=av(s * SLOT + 2048, 2048).rearrange("p (hk d) -> p hk d", d=1024),
                            in_=src2).then_inc(dring2[s].h, 16)
            P.on("gpsimd", f)
            slot_loaded[s] = (dring[s], n1, dring2[s], n2)

        for eidx in range(RING):
            load_expert(eidx)

        el_mult_done = [0] * G
        f_bank_scc_gate = [rt_exp[14], rt_exp[15],
                           rt_tr[14] if rt_tr[14] else 0,
                           rt_tr[15] if rt_tr[15] else 0]
        for g in range(E // G):
            for c in range(NCH):
                pg = c % 2
                if c == 0:
                    grp_w2 = []
                    pend_loads = []
                for el in range(G):
                    eidx = g * G + el
                    s = eidx % RING
                    prevt = tc.n
                    t_cw = tc.take(auto=False)
                    t_h = [tc.take(auto=False), tc.take(auto=False)]

                    def fexp_mm(e, eidx=eidx, s=s, c=c, pg=pg, first=(c == 0),
                                wh=slot_loaded[s], mb=tuple(mm1_bank_free),
                                need_cwb=cw_bank_free[pg], prevt=prevt, t_cw=t_cw,
                                t_h0=t_h[0], cw_gate=(rt_tr[c * 4 + 3] if g == 0 else 0)):
                        if first:
                            e.wait_ge(wh[0].h, wh[1])
                        if cw_gate > 0:
                            e.wait_ge(scc.h, cw_gate)
                        if need_cwb > 0:
                            e.wait_ge(vc.h, need_cwb)
                        if prevt > 0:
                            e.wait_ge(tc.h, prevt)
                        lhsT = id_bf[:, eidx:eidx + 1].broadcast_to((128, 128))
                        e.matmul(out=pB[:, pg * 512:(pg + 1) * 512], lhsT=lhsT,
                                 rhs=cwT[:, c * 512:(c + 1) * 512], start=True,
                                 stop=True, skip_group_check=True).then_inc(tc.h, 1)
                        for hk in range(2):
                            if mb[hk] > 0:
                                e.wait_ge(scc.h, mb[hk])
                            for dk in range(NDK):
                                if dk == NDK - 1:
                                    e.wait_ge(tc.h, t_cw if hk == 0 else t_h0)
                                ins = e.matmul(
                                    out=pA[:, hk * 512:(hk + 1) * 512],
                                    lhsT=av(s * SLOT + dk * 256 + hk * 128, 128),
                                    rhs=uT[:, dk * 2048 + c * 512:dk * 2048 + (c + 1) * 512],
                                    start=(dk == 0), stop=(dk == NDK - 1),
                                    skip_group_check=True)
                            ins.then_inc(tc.h, 1)
                    P.on("tensor", fexp_mm)
                    if c == 0:
                        grp_w2.append((slot_loaded[s][2], slot_loaded[s][3]))
                    if c == NCH - 1:
                        slot_free_w1[s] = t_h[1]
                        if eidx + RING < E:
                            pend_loads.append(eidx + RING)
                    s_h = []
                    for hk in range(2):
                        pv0 = scc.take()
                        sv = scc.take()
                        s_h.append(sv)

                        def fsilu(e, hk=hk, eidx=eidx, need_t=t_h[hk],
                                  need_v=silu_free[hk], pv0=pv0):
                            e.wait_ge(tc.h, need_t)
                            if need_v > 0:
                                e.wait_ge(vc.h, need_v)
                            e.activation(out=hs_tmp[:, hk * 512:(hk + 1) * 512],
                                         in_=pA[:, hk * 512:(hk + 1) * 512],
                                         func=AF.Silu,
                                         bias=b1T_sb[:, hk * 64 + eidx:hk * 64 + eidx + 1],
                                         scale=1.0).then_inc(scc.h, 1)
                            e.wait_ge(scc.h, pv0)
                            e.activation(out=smdum[:, 0:1],
                                         in_=hs_tmp[:, hk * 512:hk * 512 + 1],
                                         func=AF.Identity, bias=0.0,
                                         scale=1.0).then_inc(scc.h, 1)
                        P.on("scalar", fsilu)
                        mm1_bank_free[hk] = pv0
                    for hk in range(2):
                        vm = vc.take()
                        hsw_dst = av(HSW_OFF + pg * 4096 + el * 1024 + hk * 512, 512)

                        def fmul(e, hk=hk, hsw_dst=hsw_dst, pg=pg, need_s=s_h[hk],
                                 need_t=t_cw, need_reuse=hsw_mm2_done[pg]):
                            e.wait_ge(scc.h, need_s)
                            e.wait_ge(tc.h, need_t)
                            if need_reuse > 0 and hk == 0:
                                e.wait_ge(tc.h, need_reuse)
                            e.tensor_tensor(out=hsw_dst,
                                            in0=hs_tmp[:, hk * 512:(hk + 1) * 512],
                                            in1=pB[:, pg * 512:(pg + 1) * 512],
                                            op=ALU.mult).then_inc(vc.h, 1)
                        P.on("vector", fmul)
                        silu_free[hk] = vm
                    cw_bank_free[pg] = vm
                    el_mult_done[el] = vm
                for half in range(2):
                    for ttp in range(2):
                        for dc in range(2):
                            r = ttp * 2 + dc
                            tt = c * 4 + half * 2 + ttp
                            pf = pF[:, r * 512:(r + 1) * 512]
                            prevt2 = tc.n
                            tm = tc.take(auto=False)

                            def fmm2(e, g=g, pg=pg, half=half, ttp=ttp, dc=dc, pf=pf,
                                     needs=tuple(el_mult_done), need_free=f_bank_free[r],
                                     prevt2=prevt2, sgate=f_bank_scc_gate[r],
                                     w2h=(tuple(grp_w2) if (c == 0 and half == 0
                                          and ttp == 0 and dc == 0) else ())):
                                for hh_, nn_ in w2h:
                                    e.wait_ge(hh_.h, nn_)
                                if sgate > 0:
                                    e.wait_ge(scc.h, sgate)
                                if need_free > 0:
                                    e.wait_ge(vc.h, need_free)
                                i = 0
                                for el in range(G):
                                    e.wait_ge(vc.h, needs[el])
                                    for hk in range(2):
                                        if i == 2 * G - 1 and prevt2 > 0:
                                            e.wait_ge(tc.h, prevt2)
                                        ins = e.matmul(
                                            out=pf,
                                            lhsT=av(HSW_OFF + pg * 4096 + el * 1024
                                                    + hk * 512 + (half * 2 + ttp) * 128, 128),
                                            rhs=av(((g * G + el) % RING) * SLOT + 2048
                                                   + hk * 1024 + dc * 512, 512),
                                            start=(i == 0), stop=(i == 2 * G - 1),
                                            skip_group_check=True)
                                        i += 1
                                ins.then_inc(tc.h, 1)
                            P.on("tensor", fmm2)
                            f_bank_scc_gate[r] = 0
                            ve = vc.take()

                            def fev(e, pf=pf, tt=tt, dc=dc, need=tm):
                                e.wait_ge(tc.h, need)
                                xt = x_sb[:, tt * 1024 + dc * 512:tt * 1024 + (dc + 1) * 512]
                                e.scalar_tensor_tensor(out=xt, in0=pf, scalar=0.5, in1=xt,
                                                       op0=ALU.mult,
                                                       op1=ALU.add).then_inc(vc.h, 1)
                            P.on("vector", fev)
                            f_bank_free[r] = ve
                hsw_mm2_done[pg] = tc.n
                if c == NCH - 1:
                    for el in range(G):
                        slot_free_w2[(g * G + el) % RING] = tc.n
                    for nxt in pend_loads:
                        load_expert(nxt)

    def emit_shared():
        f_bank_free = [0, 0, 0, 0]
        hs_free = 0
        sw2_free = 0
        sw1_free = [0, 0]
        for es in range(ES):
            for hh in range(2):
                idx = es * 2 + hh
                ping = idx % 2
                n1 = dsw[ping].take(16)
                n2 = dsw[2].take(16)

                pp1 = dsw[ping].n - 16
                pp2 = dsw[2].n - 16

                def fload(e, es=es, hh=hh, ping=ping, need1=sw1_free[ping],
                          need2=sw2_free, pp1=pp1, pp2=pp2):
                    if need1 > 0:
                        e.wait_ge(tc.h, need1)
                    if pp1 > 0:
                        e.wait_ge(dsw[ping].h, pp1)
                    src1 = sw1[es:es + 1, :, hh * 1024:(hh + 1) * 1024].rearrange(
                        "e2 (dk p) h -> p (e2 dk) h", p=128)
                    e.dma_start(out=av(ping * 8192, 8192).rearrange(
                        "p (dk h) -> p dk h", h=1024), in_=src1).then_inc(dsw[ping].h, 16)
                    if need2 > 0:
                        e.wait_ge(tc.h, need2)
                    if pp2 > 0:
                        e.wait_ge(dsw[2].h, pp2)
                    src2 = sw2[es, hh * 1024:(hh + 1) * 1024, :].rearrange(
                        "(hk p) d -> p hk d", p=128)
                    e.dma_start(out=av(16384, 8192).rearrange(
                        "p (hk d) -> p hk d", d=1024), in_=src2).then_inc(dsw[2].h, 16)
                P.on("gpsimd", fload)
                mm1_bank_free = [0, 0]
                for th in range(2):
                    hs_gate = hs_free
                    for hkl in range(8):
                        for cc in range(2):
                            pg = (hkl * 2 + cc) % 2
                            prevt = tc.n
                            tm = tc.take(auto=False)

                            def fmm1(e, hkl=hkl, cc=cc, th=th, pg=pg, ping=ping,
                                     first=(th == 0 and hkl == 0 and cc == 0), wl=n1,
                                     need_b=mm1_bank_free[pg], prevt=prevt):
                                if first:
                                    e.wait_ge(dsw[ping].h, wl)
                                if need_b > 0:
                                    e.wait_ge(scc.h, need_b)
                                for dk in range(NDK):
                                    if dk == NDK - 1 and prevt > 0:
                                        e.wait_ge(tc.h, prevt)
                                    ins = e.matmul(
                                        out=pA[:, pg * 512:(pg + 1) * 512],
                                        lhsT=av(ping * 8192 + dk * 1024 + hkl * 128, 128),
                                        rhs=uT[:, dk * 2048 + th * 1024 + cc * 512:
                                               dk * 2048 + th * 1024 + (cc + 1) * 512],
                                        start=(dk == 0), stop=(dk == NDK - 1),
                                        skip_group_check=True)
                                ins.then_inc(tc.h, 1)
                            P.on("tensor", fmm1)
                            pv0 = scc.take()
                            sv = scc.take()

                            def fsilu(e, hkl=hkl, cc=cc, hh=hh, es=es, pg=pg, need=tm,
                                      gate=(hs_gate if (hkl == 0 and cc == 0) else 0),
                                      pv0=pv0):
                                e.wait_ge(tc.h, need)
                                if gate > 0:
                                    e.wait_ge(tc.h, gate)
                                hkg = hh * 8 + hkl
                                e.activation(
                                    out=av(24576 + hkl * 1024 + cc * 512, 512),
                                    in_=pA[:, pg * 512:(pg + 1) * 512], func=AF.Silu,
                                    bias=sb1T_sb[:, hkg * 2 + es:hkg * 2 + es + 1],
                                    scale=1.0).then_inc(scc.h, 1)
                                e.wait_ge(scc.h, pv0)
                                e.activation(out=smdum[:, 0:1],
                                             in_=av(24576 + hkl * 1024 + cc * 512, 1),
                                             func=AF.Identity, bias=0.0,
                                             scale=1.0).then_inc(scc.h, 1)
                            P.on("scalar", fsilu)
                            mm1_bank_free[pg] = pv0
                    silu_all = scc.n
                    last_terms = (es == ES - 1 and hh == 1)
                    for ttl in range(8):
                        tt = th * 8 + ttl
                        for dc in range(2):
                            r = (ttl * 2 + dc) % 4
                            pf = pF[:, r * 512:(r + 1) * 512]
                            prevt = tc.n
                            tm = tc.take(auto=False)

                            def fmm2(e, tt=tt, ttl=ttl, dc=dc, pf=pf, need_s=silu_all,
                                     need_f=f_bank_free[r], wl=n2,
                                     first=(ttl == 0 and dc == 0), lt=last_terms,
                                     prevt=prevt):
                                e.wait_ge(scc.h, need_s)
                                if first:
                                    e.wait_ge(dsw[2].h, wl)
                                if need_f > 0:
                                    e.wait_ge(vc.h, need_f)
                                for hkl in range(8):
                                    if hkl == 7 and not lt and prevt > 0:
                                        e.wait_ge(tc.h, prevt)
                                    ins = e.matmul(
                                        out=pf,
                                        lhsT=av(24576 + hkl * 1024 + ttl * 128, 128),
                                        rhs=av(16384 + hkl * 1024 + dc * 512, 512),
                                        start=(hkl == 0), stop=(hkl == 7 and not lt),
                                        skip_group_check=True)
                                if lt:
                                    if prevt > 0:
                                        e.wait_ge(tc.h, prevt)
                                    ins = e.matmul(out=pf, lhsT=cwT[:, tt * 128:(tt + 1) * 128],
                                                   rhs=b2_sb[:, dc * 512:(dc + 1) * 512],
                                                   start=False, stop=True,
                                                   skip_group_check=True)
                                ins.then_inc(tc.h, 1)
                            P.on("tensor", fmm2)
                            ve = vc.take()

                            def fev(e, pf=pf, tt=tt, dc=dc, need=tm):
                                e.wait_ge(tc.h, need)
                                xt = x_sb[:, tt * 1024 + dc * 512:tt * 1024 + (dc + 1) * 512]
                                e.scalar_tensor_tensor(out=xt, in0=pf, scalar=0.5, in1=xt,
                                                       op0=ALU.mult,
                                                       op1=ALU.add).then_inc(vc.h, 1)
                            P.on("vector", fev)
                            f_bank_free[r] = ve
                    hs_free = tc.n
                sw1_free[ping] = tc.n
                sw2_free = tc.n

    def emit_attention():
        KN = 16384
        WA, WB = 0, 8192

        def load_w(off, src, gate=0):
            pp = dsw[3].n
            n = dsw[3].take(16)

            def f(e, off=off, src=src, gate=gate, pp=pp):
                if gate > 0:
                    e.wait_ge(tc.h, gate)
                if pp > 0:
                    e.wait_ge(dsw[3].h, pp)
                e.dma_start(out=av(off, 8192).rearrange("p (dk n2) -> p dk n2", n2=1024),
                            in_=src.rearrange("(dk p) n2 -> p dk n2", p=128)).then_inc(dsw[3].h, 16)
            P.on("gpsimd", f)
            return n

        wk_h = load_w(WA, w_k)
        wv_h = load_w(WB, w_v)
        kland_pf = pF[:, 0:512]

        def PQ(par):
            return pA if par == 0 else pB

        def softmax_chain(tt, pre_done, out_f16, accq, norm_gate=0):
            par = tt % 2
            P.wait("vector", tc, pre_done)
            if accq[0] > 0:
                P.wait("vector", scc, accq[0])
            vm = vc.take()
            P.on("vector", lambda e, par=par: e.reduce_max(
                out=sm[:, par:par + 1], in_=PQ(par)[:, :], axis=AX.X,
                negate=True).then_inc(vc.h, 1))
            P.wait("scalar", vc, vm)
            if accq[1] > 0:
                P.wait("scalar", vc, accq[1])
            pv0 = scc.take()
            sd = scc.take()

            def fexp(e, par=par, pv0=pv0):
                e.activation(out=scr[:, par * 1024:(par + 1) * 1024], in_=PQ(par)[:, :],
                             func=AF.Exp, bias=sm[:, par:par + 1], scale=1.0,
                             accum_out=sm[:, 2 + par:3 + par]).then_inc(scc.h, 1)
                e.wait_ge(scc.h, pv0)
                e.activation(out=smdum[:, 0:1], in_=sm[:, 2 + par:3 + par],
                             func=AF.Identity, bias=0.0, scale=1.0).then_inc(scc.h, 1)
            P.on("scalar", fexp)
            accq[0] = sd
            P.wait("vector", scc, sd)
            v1 = vc.take()
            P.on("vector", lambda e, par=par: e.reciprocal(
                out=sm[:, 4 + par:5 + par], in_=sm[:, 2 + par:3 + par]).then_inc(vc.h, 1))
            P.wait("vector", vc, v1)
            if norm_gate > 0:
                P.wait("vector", tc, norm_gate)
            v2 = vc.take()
            P.on("vector", lambda e, par=par, out_f16=out_f16: e.tensor_scalar(
                out=out_f16, in0=scr[:, par * 1024:(par + 1) * 1024],
                scalar1=sm[:, 4 + par:5 + par], scalar2=None,
                op0=ALU.mult).then_inc(vc.h, 1))
            accq[1] = v2
            return sd, v2

        def pre_mm(tt, woff, bias_sb, gate_scc, gate_vc, wh):
            par = tt % 2
            prevt = tc.n
            tp = tc.take(auto=False)

            def fpre(e, tt=tt, par=par, woff=woff, bias_sb=bias_sb, wh=wh,
                     gs=gate_scc, gv=gate_vc, prevt=prevt):
                if wh is not None:
                    e.wait_ge(dsw[3].h, wh)
                if gs > 0:
                    e.wait_ge(scc.h, gs)
                if gv > 0:
                    e.wait_ge(vc.h, gv)
                for dc in range(2):
                    for dk in range(NDK):
                        e.matmul(out=PQ(par)[:, dc * 512:(dc + 1) * 512],
                                 lhsT=uT[:, dk * 2048 + tt * 128:dk * 2048 + (tt + 1) * 128],
                                 rhs=av(woff + dk * 1024 + dc * 512, 512),
                                 start=(dk == 0), stop=False, skip_group_check=True)
                    if dc == 1 and prevt > 0:
                        e.wait_ge(tc.h, prevt)
                    ins = e.matmul(out=PQ(par)[:, dc * 512:(dc + 1) * 512],
                                   lhsT=ones1[:, :],
                                   rhs=bias_sb[:, dc * 512:(dc + 1) * 512], start=False,
                                   stop=True, skip_group_check=True)
                ins.then_inc(tc.h, 1)
            P.on("tensor", fpre)
            return tp

        exp_par = [[0, 0], [0, 0]]
        norm_done = [0] * NT
        kland_done = 0
        pre_done = [0] * NT
        for tt in range(NT):
            par = tt % 2
            gate = exp_par[par][0]
            pre_done[tt] = pre_mm(tt, WA, bk_sb, gate, 0, wk_h if tt == 0 else None)
            kn = av16(KN + tt * 1024, 1024)
            sd, v2 = softmax_chain(tt, pre_done[tt], kn, exp_par[par])
            norm_done[tt] = v2
            prevt = tc.n
            td = tc.take(auto=False)

            def fkl(e, tt=tt, kn=kn, need=v2, prevt=prevt):
                e.wait_ge(vc.h, need)
                for dk in range(NDK):
                    if dk == NDK - 1 and prevt > 0:
                        e.wait_ge(tc.h, prevt)
                    ins = e.matmul(out=kland_pf[:, dk * 64:(dk + 1) * 64],
                                   lhsT=kn[:, dk * 128:(dk + 1) * 128],
                                   rhs=apat[:, tt * 64:(tt + 1) * 64],
                                   start=(tt == 0 and dk == 0),
                                   stop=(tt == NT - 1 and dk == NDK - 1),
                                   skip_group_check=True)
                ins.then_inc(tc.h, 1)
            P.on("tensor", fkl)
            kland_done = td
        P.wait("scalar", tc, kland_done)
        kl0 = scc.take()
        klT_d = scc.take()

        def fklev(e, kl0=kl0):
            e.copy(out=klT[:, :], in_=kland_pf).then_inc(scc.h, 1)
            e.wait_ge(scc.h, kl0)
            e.activation(out=smdum[:, 0:1], in_=klT[:, 0:1], func=AF.Identity,
                         bias=0.0, scale=1.0).then_inc(scc.h, 1)
        P.on("scalar", fklev)

        wq_h = load_w(WA, w_q, gate=kland_done)

        mid_pf0 = pF[0:64, 0:512]
        mid_pf1 = pF[0:64, 512:1024]
        exp_par = [[0, 0], [0, 0]]
        mid_tc = [0] * NT
        k3_done = [0] * NT
        kts_done = [0] * NT
        k2s_done = [0] * NT
        tr_bank_free = [0, 0]
        k2_bank_free = [0, 0]
        mid_done = 0

        def a2_tail(tt):
            nonlocal mid_done
            par = tt % 2
            kn = av16(KN + tt * 1024, 1024)
            ptb = pF[:, 1024 + par * 512:1024 + (par + 1) * 512].bitcast(F16)
            prevt = tc.n
            td = tc.take(auto=False)

            def ftr(e, kn=kn, ptb=ptb, prevt=prevt, g1=tr_bank_free[par],
                    g2=k2_bank_free[par]):
                if g1 > 0:
                    e.wait_ge(scc.h, g1)
                if g2 > 0:
                    e.wait_ge(scc.h, g2)
                for dk in range(NDK):
                    if dk == NDK - 1 and prevt > 0:
                        e.wait_ge(tc.h, prevt)
                    ins = e.transpose(out=ptb[:, dk * 128:(dk + 1) * 128],
                                      in_=kn[:, dk * 128:(dk + 1) * 128],
                                      identity=id_f16[:, :])
                ins.then_inc(tc.h, 1)
            P.on("tensor", ftr)
            P.wait("scalar", tc, td)
            kt0 = scc.take()
            ktsd = scc.take()
            kts = stg[:, 2048 + par * 1024:2048 + (par + 1) * 1024].bitcast(F16)

            def fktsev(e, ptb=ptb, kts=kts, kt0=kt0):
                e.copy(out=kts, in_=ptb).then_inc(scc.h, 1)
                e.wait_ge(scc.h, kt0)
                e.activation(out=smdum[:, 0:1], in_=kts[:, 0:1],
                             func=AF.Identity, bias=0.0, scale=1.0).then_inc(scc.h, 1)
            P.on("scalar", fktsev)
            kts_done[tt] = ktsd
            tr_bank_free[par] = ktsd
            k2o = pF[:, 1024 + par * 512:1024 + par * 512 + 64]
            prevt = tc.n
            tk2 = tc.take(auto=False)

            def fk2(e, kts=kts, k2o=k2o, need_s=ktsd, need_kl=klT_d, prevt=prevt):
                e.wait_ge(scc.h, max(need_s, need_kl))
                for dk in range(NDK):
                    if dk == NDK - 1 and prevt > 0:
                        e.wait_ge(tc.h, prevt)
                    ins = e.matmul(out=k2o, lhsT=kts[:, dk * 128:(dk + 1) * 128],
                                   rhs=klT[:, dk * 64:(dk + 1) * 64], start=(dk == 0),
                                   stop=(dk == NDK - 1), skip_group_check=True)
                ins.then_inc(tc.h, 1)
            P.on("tensor", fk2)
            P.wait("scalar", tc, tk2)
            k20 = scc.take()
            k2sd = scc.take()

            def fk2ev(e, k2o=k2o, par=par, k20=k20):
                e.copy(out=k2s[:, par * 64:(par + 1) * 64], in_=k2o).then_inc(scc.h, 1)
                e.wait_ge(scc.h, k20)
                e.activation(out=smdum[:, 0:1], in_=k2s[:, par * 64:par * 64 + 1],
                             func=AF.Identity, bias=0.0, scale=1.0).then_inc(scc.h, 1)
            P.on("scalar", fk2ev)
            k2s_done[tt] = k2sd
            k2_bank_free[par] = k2sd
            k3 = stg[:, par * 1024:(par + 1) * 1024].bitcast(F16)
            prevt = tc.n
            tmid = tc.take(auto=False)

            def fmid(e, k3=k3, tt=tt, par=par, need_s=k2sd, need_v=k3_done[tt],
                     need_kl=klT_d, prevt=prevt):
                e.wait_ge(scc.h, max(need_s, need_kl))
                e.wait_ge(vc.h, need_v)
                for dc, pfm in ((0, mid_pf0), (1, mid_pf1)):
                    if dc == 1 and prevt > 0:
                        e.wait_ge(tc.h, prevt)
                    ins = e.matmul(out=pfm, lhsT=k2s[:, par * 64:(par + 1) * 64],
                                   rhs=k3[:, dc * 512:(dc + 1) * 512],
                                   start=(tt == 0), stop=(tt == NT - 1),
                                   skip_group_check=True)
                ins.then_inc(tc.h, 1)
            P.on("tensor", fmid)
            mid_tc[tt] = tmid
            mid_done = tmid

        exp_parA = exp_par
        for tt in range(NT):
            par = tt % 2
            gate_v = k3_done[tt - 2] if tt >= 2 else 0
            pre_done[tt] = pre_mm(tt, WB, bv_sb, exp_parA[par][0], gate_v,
                                  wv_h if tt == 0 else None)
            exp_parA[par][0] = 0
            P.wait("vector", tc, pre_done[tt])
            if tt >= 2:
                P.wait("vector", tc, mid_tc[tt - 2])
            kn = av16(KN + tt * 1024, 1024)
            k3 = stg[:, par * 1024:(par + 1) * 1024].bitcast(F16)
            v3 = vc.take()
            P.on("vector", lambda e, kn=kn, k3=k3, par=par: e.tensor_tensor(
                out=k3, in0=kn, in1=PQ(par)[:, :], op=ALU.mult).then_inc(vc.h, 1))
            k3_done[tt] = v3
            if tt >= 1:
                a2_tail(tt - 1)
        a2_tail(NT - 1)
        P.wait("scalar", tc, mid_done)
        md0 = scc.take()
        md1 = scc.take()
        mid_d = scc.take()

        def fmidev(e, md0=md0, md1=md1):
            e.copy(out=midb[:, 0:512], in_=mid_pf0).then_inc(scc.h, 1)
            e.wait_ge(scc.h, md0)
            e.copy(out=midb[:, 512:1024], in_=mid_pf1).then_inc(scc.h, 1)
            e.wait_ge(scc.h, md1)
            e.activation(out=smdum[0:64, 0:1], in_=midb[:, 0:1],
                         func=AF.Identity, bias=0.0, scale=1.0).then_inc(scc.h, 1)
        P.on("scalar", fmidev)

        exp_par = [[0, 0], [0, 0]]
        qn_done = [0] * NT
        out_free = [0, 0]
        trq_free = [0, 0]
        k1_bank_free = [0, 0]
        trq_done = [0] * NT

        def b_tail(tt):
            par = tt % 2
            qn = stg[:, par * 1024:(par + 1) * 1024].bitcast(F16)
            ptb = pF[:, 1024 + par * 512:1024 + (par + 1) * 512].bitcast(F16)
            prevt = tc.n
            td = tc.take(auto=False)

            def ftrq(e, qn=qn, ptb=ptb, prevt=prevt, need_v=qn_done[tt],
                     g1=trq_free[par], g2=k1_bank_free[par]):
                e.wait_ge(vc.h, need_v)
                if g1 > 0:
                    e.wait_ge(scc.h, g1)
                if g2 > 0:
                    e.wait_ge(scc.h, g2)
                for dk in range(NDK):
                    if dk == NDK - 1 and prevt > 0:
                        e.wait_ge(tc.h, prevt)
                    ins = e.transpose(out=ptb[:, dk * 128:(dk + 1) * 128],
                                      in_=qn[:, dk * 128:(dk + 1) * 128],
                                      identity=id_f16[:, :])
                ins.then_inc(tc.h, 1)
            P.on("tensor", ftrq)
            trq_done[tt] = td
            P.wait("scalar", tc, td)
            qt0 = scc.take()
            qtsd = scc.take()
            qts = stg[:, 2048 + par * 1024:2048 + (par + 1) * 1024].bitcast(F16)

            def fqtsev(e, ptb=ptb, qts=qts, qt0=qt0):
                e.copy(out=qts, in_=ptb).then_inc(scc.h, 1)
                e.wait_ge(scc.h, qt0)
                e.activation(out=smdum[:, 0:1], in_=qts[:, 0:1],
                             func=AF.Identity, bias=0.0, scale=1.0).then_inc(scc.h, 1)
            P.on("scalar", fqtsev)
            trq_free[par] = qtsd
            k1o = pF[0:64, 1024 + par * 512:1024 + par * 512 + 128]
            prevt = tc.n
            tk1 = tc.take(auto=False)

            def fk1(e, qts=qts, k1o=k1o, need=qtsd, need_kl=klT_d, prevt=prevt):
                e.wait_ge(scc.h, max(need, need_kl))
                for dk in range(NDK):
                    if dk == NDK - 1 and prevt > 0:
                        e.wait_ge(tc.h, prevt)
                    ins = e.matmul(out=k1o,
                                   lhsT=klT[:, dk * 64:(dk + 1) * 64],
                                   rhs=qts[:, dk * 128:(dk + 1) * 128], start=(dk == 0),
                                   stop=(dk == NDK - 1), skip_group_check=True)
                ins.then_inc(tc.h, 1)
            P.on("tensor", fk1)
            P.wait("scalar", tc, tk1)
            k10 = scc.take()
            k1sd = scc.take()

            def fk1ev(e, k1o=k1o, par=par, k10=k10):
                e.copy(out=k1s[:, par * 128:(par + 1) * 128], in_=k1o).then_inc(scc.h, 1)
                e.wait_ge(scc.h, k10)
                e.activation(out=smdum[0:64, 0:1], in_=k1s[:, par * 128:par * 128 + 1],
                             func=AF.Identity, bias=0.0, scale=1.0).then_inc(scc.h, 1)
            P.on("scalar", fk1ev)
            k1_bank_free[par] = k1sd
            for dc in range(2):
                po = pF[:, dc * 512:(dc + 1) * 512]
                prevt = tc.n
                to = tc.take(auto=False)

                def fo(e, po=po, par=par, dc=dc, need_s=k1sd, need_m=mid_d,
                       need_f=out_free[dc], prevt=prevt):
                    e.wait_ge(scc.h, max(need_s, need_m))
                    if need_f > 0:
                        e.wait_ge(vc.h, need_f)
                    if prevt > 0:
                        e.wait_ge(tc.h, prevt)
                    e.matmul(out=po, lhsT=k1s[:, par * 128:(par + 1) * 128],
                             rhs=midb[:, dc * 512:(dc + 1) * 512], start=True,
                             stop=True, skip_group_check=True).then_inc(tc.h, 1)
                P.on("tensor", fo)
                vo = vc.take()

                def foev(e, po=po, tt=tt, dc=dc, need=to):
                    e.wait_ge(tc.h, need)
                    xt = x_sb[:, tt * 1024 + dc * 512:tt * 1024 + (dc + 1) * 512]
                    e.tensor_tensor(out=xt, in0=po, in1=xt,
                                    op=ALU.add).then_inc(vc.h, 1)
                P.on("vector", foev)
                out_free[dc] = vo

        for tt in range(NT):
            par = tt % 2
            pq_gate_v = k3_done[14 + par] if tt < 2 else 0
            pre_done[tt] = pre_mm(tt, WA, bq_sb, exp_par[par][0], pq_gate_v,
                                  wq_h if tt == 0 else None)
            qn = stg[:, par * 1024:(par + 1) * 1024].bitcast(F16)
            ng = trq_done[tt - 2] if tt >= 2 else mid_tc[14 + par]
            sd, v2 = softmax_chain(tt, pre_done[tt], qn, exp_par[par], norm_gate=ng)
            qn_done[tt] = v2
            if tt >= 1:
                b_tail(tt - 1)
        b_tail(NT - 1)

    def ph_store():
        P.wait("sync", vc)
        no = dout.take(64)

        def f(e):
            xs = x_out.rearrange("(tt p) d -> p tt d", p=128)
            for i in range(4):
                if i > 0:
                    e.wait_ge(dout.h, 16 * i)
                src = x_sb[:, i * 4096:(i + 1) * 4096].rearrange("p (tt d) -> p tt d", d=1024)
                e.dma_start(out=xs[:, i * 4:(i + 1) * 4, :], in_=src).then_inc(dout.h, 16)
            e.wait_ge(dout.h, no)
        P.on("sync", f)

    emitters = {"ln1": emit_ln, "rt1": emit_router,
                "rt1a": (lambda: emit_router(1)), "rt1b": (lambda: emit_router(2)),
                "m1": (lambda: emit_routed(1)), "m2": (lambda: emit_routed(2)),
                "m3": (lambda: emit_routed(3)),
                "moe1": emit_routed,
                "sh1": emit_shared, "ln2": emit_ln, "attn": emit_attention,
                "ln3": emit_ln, "rt2": emit_router, "moe2": emit_routed,
                "sh2": emit_shared}
    ph_load()
    join()
    for i, ph in enumerate(phases):
        emitters[ph]()
        nxt = phases[i + 1] if i + 1 < len(phases) else None
        if ph.startswith("rt") and nxt is not None and nxt.startswith("moe"):
            continue
        join()
    ph_store()
    if dbg:
        no2 = dout.take(16 * 6)

        def fdbg(e):
            e.wait_ge(dout.h, 64)
            e.dma_start(out=dbg_uT, in_=uT[:, :]).then_inc(dout.h, 16)
            e.wait_ge(dout.h, 80)
            e.dma_start(out=dbg_cw, in_=cw05[:, :]).then_inc(dout.h, 16)
            e.wait_ge(dout.h, 96)
            e.dma_start(out=dbg_cwT, in_=cwT[0:64, :]).then_inc(dout.h, 16)
            e.wait_ge(dout.h, 112)
            e.dma_start(out=dbg_klT, in_=klT[:, :]).then_inc(dout.h, 16)
            e.wait_ge(dout.h, 128)
            e.dma_start(out=dbg_mid, in_=midb[:, :]).then_inc(dout.h, 16)
            e.wait_ge(dout.h, 144)
            e.dma_start(out=dbg_kn, in_=arena[:, 16384:32768].bitcast(F16)).then_inc(dout.h, 16)
            e.wait_ge(dout.h, no2)
        P.on("sync", fdbg)

    with nc.Block() as blk:
        for name in ENGINES:
            def runner(e, fns=tuple(P.q[name])):
                for fn in fns:
                    fn(e)
            getattr(blk, name)(runner)

    stack.close()
    return nc


_NC_CACHE = None


def _consts():
    ident = np.eye(128)
    apat = np.zeros((128, NT * L), np.float16)
    for tt in range(NT):
        for p in range(128):
            li = (tt * 128 + p) // 32
            apat[p, tt * L + li] = 1.0 / 32.0
    return dict(
        c_idbf=ident.astype(ml_dtypes.bfloat16),
        c_idf16=ident.astype(np.float16),
        c_idf32=ident.astype(np.float32),
        c_id64=np.eye(64).astype(np.float16),
        c_ones1=np.ones((1, 128), np.float16),
        c_ones2=np.ones((2, 128), np.float16),
        c_apat=apat,
    )


def kernel(L0_attn_hidden_states, L0_attn_Wq, L0_attn_bq, L0_attn_Wk, L0_attn_bk,
           L0_attn_Wv, L0_attn_bv, L0_ffn_router_w, L0_ffn_router_b, L0_ffn_w1,
           L0_ffn_b1, L0_ffn_w2, L0_ffn_b2, L0_ffn_sw1, L0_ffn_sb1, L0_ffn_sw2,
           L0_ffn_sb2):
    global _NC_CACHE
    if _NC_CACHE is None:
        _NC_CACHE = build_nc()
    nc = _NC_CACHE

    xs = np.ascontiguousarray(np.asarray(L0_attn_hidden_states, dtype=np.float32))
    base = dict(
        wq=np.ascontiguousarray(L0_attn_Wq, np.float32),
        wk=np.ascontiguousarray(L0_attn_Wk, np.float32),
        wv=np.ascontiguousarray(L0_attn_Wv, np.float32),
        bq=np.ascontiguousarray(L0_attn_bq, np.float32).reshape(1, D),
        bk=np.ascontiguousarray(L0_attn_bk, np.float32).reshape(1, D),
        bv=np.ascontiguousarray(L0_attn_bv, np.float32).reshape(1, D),
        router_w=np.ascontiguousarray(L0_ffn_router_w, np.float32),
        router_b=np.ascontiguousarray(L0_ffn_router_b, np.float32).reshape(1, E),
        w1=np.ascontiguousarray(L0_ffn_w1, np.float32),
        b1t=np.ascontiguousarray(np.asarray(L0_ffn_b1, dtype=np.float32).T),
        w2=np.ascontiguousarray(L0_ffn_w2, np.float32),
        b2=np.ascontiguousarray(L0_ffn_b2, np.float32),
        sw1=np.ascontiguousarray(L0_ffn_sw1, np.float32),
        sb1t=np.ascontiguousarray(np.asarray(L0_ffn_sb1, dtype=np.float32).T),
        sw2=np.ascontiguousarray(L0_ffn_sw2, np.float32),
        sb2=np.ascontiguousarray(L0_ffn_sb2, np.float32),
    )
    base.update(_consts())
    in_maps = []
    for b in range(B):
        m = dict(base)
        m["x"] = np.ascontiguousarray(xs[b])
        in_maps.append(m)
    res = run_bass_kernel_spmd(nc, in_maps, core_ids=list(range(B)))
    out = np.stack([np.asarray(res.results[b]["out"]) for b in range(B)], axis=0)
    return out.astype(np.float32)


# version 10
# speedup vs baseline: 1.0482x; 1.0482x over previous
import numpy as np
import ml_dtypes
import contextlib
import concourse.bass as bass
import concourse.mybir as mybir
from concourse.bass_utils import run_bass_kernel_spmd

F32 = mybir.dt.float32
BF16 = mybir.dt.bfloat16
F16 = mybir.dt.float16
AF = mybir.ActivationFunctionType
ALU = mybir.AluOpType
AX = mybir.AxisListType

B, S, D = 8, 2048, 1024
E, TOPK, H = 64, 8, 256
ES, HS = 2, 2048
L = 64
NT = S // 128
NDK = D // 128
NCH = 4
G = 4
RING = 6
SLOT = 4096
HSW_OFF = RING * SLOT
EPS = 1e-5

ENGINES = ("sync", "gpsimd", "tensor", "vector", "scalar")


class Ctr:
    def __init__(self, nc, stack, name, prog=None, eng=None):
        self.h = stack.enter_context(nc.semaphore(name))
        self.n = 0
        self.prog = prog
        self.eng = eng

    def take(self, k=1, auto=True):
        prev = self.n
        self.n += k
        if auto and self.prog is not None and self.eng is not None:
            if self.prog.pending[self.eng] is None:
                self.prog.pending[self.eng] = (self.h, prev)
        return self.n


class Prog:
    def __init__(self):
        self.q = {e: [] for e in ENGINES}
        self.pending = {e: None for e in ENGINES}

    def on(self, eng, fn):
        p = self.pending[eng]
        if p is not None:
            self.pending[eng] = None
            if p[1] > 0:
                self.q[eng].append(lambda e, h=p[0], v=p[1]: e.wait_ge(h, v))
        self.q[eng].append(fn)

    def wait(self, eng, ctr, val=None):
        v = ctr.n if val is None else val
        assert v <= ctr.n, "waiting on a future milestone"
        if v > 0:
            self.q[eng].append(lambda e, h=ctr.h, v=v: e.wait_ge(h, v))


def build_nc(phases=None, dbg=False):
    if phases is None:
        phases = ["ln1", "rt1", "moe1", "sh1", "ln2", "attn", "ln3", "rt2", "moe2", "sh2"]
    nc = bass.Bass("TRN2", target_bir_lowering=False, debug=False)
    dr = {}

    def din(name, shape, dt=F32):
        dr[name] = nc.dram_tensor(name, list(shape), dt, kind="ExternalInput").ap()
        return dr[name]

    x_in = din("x", (S, D))
    w_q = din("wq", (D, D))
    w_k = din("wk", (D, D))
    w_v = din("wv", (D, D))
    b_q = din("bq", (1, D))
    b_k = din("bk", (1, D))
    b_v = din("bv", (1, D))
    w_r = din("router_w", (D, E))
    b_r = din("router_b", (1, E))
    w1 = din("w1", (E, D, H))
    b1t = din("b1t", (H, E))
    w2 = din("w2", (E, H, D))
    b2 = din("b2", (E, D))
    sw1 = din("sw1", (ES, D, HS))
    sb1t = din("sb1t", (HS, ES))
    sw2 = din("sw2", (ES, HS, D))
    sb2 = din("sb2", (ES, D))
    c_idbf = din("c_idbf", (128, 128), BF16)
    c_idf16 = din("c_idf16", (128, 128), F16)
    c_idf32 = din("c_idf32", (128, 128), F32)
    c_id64 = din("c_id64", (64, 64), F16)
    c_ones1 = din("c_ones1", (1, 128), F16)
    c_ones2 = din("c_ones2", (2, 128), F16)
    c_apat = din("c_apat", (128, NT * L), F16)
    x_out = nc.dram_tensor("out", [S, D], F32, kind="ExternalOutput").ap()
    if dbg:
        dbg_uT = nc.dram_tensor("dbg_uT", [128, NDK * 2048], BF16, kind="ExternalOutput").ap()
        dbg_cw = nc.dram_tensor("dbg_cw", [128, NT * 64], F32, kind="ExternalOutput").ap()
        dbg_cwT = nc.dram_tensor("dbg_cwT", [64, S], BF16, kind="ExternalOutput").ap()
        dbg_klT = nc.dram_tensor("dbg_klT", [128, NDK * L], F16, kind="ExternalOutput").ap()
        dbg_mid = nc.dram_tensor("dbg_mid", [64, D], F16, kind="ExternalOutput").ap()
        dbg_kn = nc.dram_tensor("dbg_kn", [128, 16384], F16, kind="ExternalOutput").ap()

    stack = contextlib.ExitStack()

    def sb(name, shape, dt):
        return stack.enter_context(nc.sbuf_tensor(name, list(shape), dt))

    x_sb = sb("x_sb", (128, NT * 1024), F32)
    uT = sb("uT", (128, NDK * 2048), BF16)
    arena = sb("arena", (128, 32768), BF16)
    hs_tmp = sb("hs_tmp", (128, 2 * 512), F32)
    scr = sb("scr", (128, 2 * 1024), F32)
    stg = sb("stg", (128, 4 * 1024), BF16)
    cw05 = sb("cw05", (128, NT * 64), F32)
    cwT = sb("cwT", (128, S), BF16)
    sm = sb("sm", (128, 64), F32)
    smeps = sb("smeps", (128, 1), F32)
    smdum = sb("smdum", (128, 4), F32)
    klT = sb("klT", (128, NDK * L), F16)
    midb = sb("midb", (64, D), F16)
    k1s = sb("k1s", (64, 256), F16)
    k2s = sb("k2s", (128, 128), F16)
    b1T_sb = sb("b1T_sb", (128, 2 * 64), F32)
    sb1T_sb = sb("sb1T_sb", (128, 16 * 2), F32)
    b2_sb = sb("b2_sb", (128, D), F16)
    wr_sb = sb("wr_sb", (128, NDK * E), BF16)
    rb_sb = sb("rb_sb", (1, E), F16)
    bq_sb = sb("bq_sb", (1, D), F16)
    bk_sb = sb("bk_sb", (1, D), F16)
    bv_sb = sb("bv_sb", (1, D), F16)
    id_bf = sb("id_bf", (128, 128), BF16)
    id_f16 = sb("id_f16", (128, 128), F16)
    id_f32 = sb("id_f32", (128, 128), F32)
    id64 = sb("id64", (64, 64), F16)
    ones1 = sb("ones1", (1, 128), F16)
    apat = sb("apat", (128, NT * L), F16)

    def av(off, n):
        return arena[:, off:off + n]

    def av16(off, n):
        return arena[:, off:off + n].bitcast(F16)

    pA = stack.enter_context(nc.psum_tensor("pA", [128, 1024], F32))
    pB = stack.enter_context(nc.psum_tensor("pB", [128, 1024], F32))
    pF = stack.enter_context(nc.psum_tensor("pF", [128, 2048], F32))

    P = Prog()
    HMS = {}
    ctr = {e: Ctr(nc, stack, f"c_{e}", prog=P, eng=e) for e in ENGINES}
    tc, vc, scc = ctr["tensor"], ctr["vector"], ctr["scalar"]
    dring = [Ctr(nc, stack, f"dr{i}") for i in range(RING)]
    dring2 = [Ctr(nc, stack, f"dr2_{i}") for i in range(RING)]
    dsw = [Ctr(nc, stack, f"dsw{i}") for i in range(4)]
    dx = Ctr(nc, stack, "dx")
    dcst = Ctr(nc, stack, "dc")
    dcstg = Ctr(nc, stack, "dcg")
    dout = Ctr(nc, stack, "do")

    def join():
        snaps = {}
        for name in ENGINES:
            c = ctr[name]
            v = c.take()
            if name == "scalar":
                P.on(name, lambda e, h=c.h: e.activation(
                    out=smdum[:, 3:4], in_=smeps[:, 0:1], func=AF.Identity,
                    bias=0.0, scale=1.0).then_inc(h, 1))
            elif name == "vector":
                P.on(name, lambda e, h=c.h: e.tensor_copy(
                    out=smdum[:, 1:2], in_=smdum[:, 2:3]).then_inc(h, 1))
            else:
                P.on(name, lambda e, h=c.h: e.nop().then_inc(h, 1))
            snaps[name] = v
        for name in ENGINES:
            for other in ENGINES:
                if other != name:
                    P.wait(name, ctr[other], snaps[other])

    def ph_load():
        xs = x_in.rearrange("(tt p) d -> p tt d", p=128)
        nx = dx.take(16 * 4)

        def f(e):
            for i in range(4):
                if i > 0:
                    e.wait_ge(dx.h, 16 * i)
                dst = x_sb[:, i * 4096:(i + 1) * 4096].rearrange("p (tt d) -> p tt d", d=1024)
                e.dma_start(out=dst, in_=xs[:, i * 4:(i + 1) * 4, :]).then_inc(dx.h, 16)
            j = 0
            for dst, src in ((id_bf, c_idbf), (id_f16, c_idf16), (id_f32, c_idf32),
                             (id64, c_id64), (ones1, c_ones1),
                             (apat, c_apat)):
                if j > 0:
                    e.wait_ge(dcst.h, 16 * j)
                e.dma_start(out=dst[:, :], in_=src).then_inc(dcst.h, 16)
                j += 1
        dcst.take(16 * 6)
        P.on("sync", f)

        def g(e):
            srcs = [None] * 9
            e.dma_start(out=wr_sb[:, :].rearrange("p (dk e2) -> p dk e2", e2=E),
                        in_=w_r.rearrange("(dk p) e2 -> p dk e2", p=128)).then_inc(dcstg.h, 16)
            j = 1
            for dst, src in ((rb_sb, b_r), (bq_sb, b_q), (bk_sb, b_k), (bv_sb, b_v)):
                e.wait_ge(dcstg.h, 16 * j)
                e.dma_start(out=dst[:, :], in_=src).then_inc(dcstg.h, 16)
                j += 1
            e.wait_ge(vc.h, 1)
            e.wait_ge(dcstg.h, 16 * j); j += 1
            e.dma_start(out=b2_sb[0:64, :], in_=b2).then_inc(dcstg.h, 16)
            e.wait_ge(dcstg.h, 16 * j); j += 1
            e.dma_start(out=b2_sb[64:66, :], in_=sb2).then_inc(dcstg.h, 16)
            e.wait_ge(dcstg.h, 16 * j); j += 1
            e.dma_start(out=b1T_sb[:, :].rearrange("p (hk e2) -> p hk e2", e2=64),
                        in_=b1t.rearrange("(hk p) e2 -> p hk e2", p=128)).then_inc(dcstg.h, 16)
            e.wait_ge(dcstg.h, 16 * j); j += 1
            e.dma_start(out=sb1T_sb[:, :].rearrange("p (hk e2) -> p hk e2", e2=2),
                        in_=sb1t.rearrange("(hk p) e2 -> p hk e2", p=128)).then_inc(dcstg.h, 16)
        dcstg.take(16 * 9)
        P.on("gpsimd", g)
        v1_ = vc.take()
        v2_ = vc.take()

        def fmemset(e):
            e.memset(smeps[:, :], EPS)
            e.memset(smdum[:, :], 0.0)
            e.memset(sm[:, :], 0.0)
            e.memset(b2_sb[:, :], 0.0)
            e.memset(cwT[:, :], 0.0).then_inc(vc.h, 1)
            e.wait_ge(vc.h, v1_)
            e.memset(cwT[64:66, :], 1.0).then_inc(vc.h, 1)
        P.on("vector", fmemset)
        for name in ("tensor", "vector", "scalar"):
            P.wait(name, vc)
        P.wait("tensor", dcst, 16)
        HMS["dx"] = dx
        HMS["dcst_all"] = dcst.n
        HMS["dcstg_all"] = dcstg.n

    def emit_ln():
        sqs = [0] * NT
        dxh = HMS.get("dx")
        for tt in range(NT):
            xt = x_sb[:, tt * 1024:(tt + 1) * 1024]
            if dxh is not None and tt % 4 == 0:
                P.wait("vector", dxh, 16 * (tt // 4 + 1))
                P.wait("scalar", dxh, 16 * (tt // 4 + 1))
            vc.take()
            P.on("vector", lambda e, xt=xt, tt=tt: e.tensor_reduce(
                out=sm[:, tt:tt + 1], in_=xt, axis=AX.X,
                op=ALU.add).then_inc(vc.h, 1))
            if tt >= 2:
                P.wait("scalar", scc, sqs[tt - 2])
            last = (tt == NT - 1)
            pv = scc.take()
            sq = scc.take() if last else pv
            sqs[tt] = sq

            def fsq(e, xt=xt, tt=tt, last=last, pv=pv):
                e.activation(
                    out=scr[:, (tt % 2) * 1024:(tt % 2 + 1) * 1024], in_=xt,
                    func=AF.Square, bias=0.0, scale=1.0,
                    accum_out=sm[:, 16 + tt:17 + tt]).then_inc(scc.h, 1)
                if last:
                    e.wait_ge(scc.h, pv)
                    e.activation(out=smdum[:, 0:1], in_=sm[:, 16 + tt:17 + tt],
                                 func=AF.Identity, bias=0.0, scale=1.0).then_inc(scc.h, 1)
            P.on("scalar", fsq)
        v0 = vc.n
        P.wait("vector", vc, v0)
        P.wait("vector", scc)
        v1 = vc.take()

        def stats(e):
            e.tensor_scalar(out=sm[:, 32:48], in0=sm[:, 0:16], scalar1=1.0 / D,
                            scalar2=None, op0=ALU.mult)
            e.tensor_scalar(out=sm[:, 16:32], in0=sm[:, 16:32], scalar1=1.0 / D,
                            scalar2=None, op0=ALU.mult).then_inc(vc.h, 1)
        P.on("vector", stats)
        P.wait("vector", vc, v1)
        v2 = vc.take()
        P.on("vector", lambda e: e.tensor_tensor(
            out=sm[:, 48:64], in0=sm[:, 32:48], in1=sm[:, 32:48],
            op=ALU.mult).then_inc(vc.h, 1))
        P.wait("vector", vc, v2)
        v3 = vc.take()
        P.on("vector", lambda e: e.tensor_tensor(
            out=sm[:, 48:64], in0=sm[:, 16:32], in1=sm[:, 48:64],
            op=ALU.subtract).then_inc(vc.h, 1))
        P.wait("scalar", vc, v3)
        s0 = scc.take()
        s1 = scc.take()

        def fsqrt(e, s0=s0):
            e.activation(out=sm[:, 48:64], in_=sm[:, 48:64], func=AF.Sqrt,
                         bias=smeps[:, 0:1], scale=1.0).then_inc(scc.h, 1)
            e.wait_ge(scc.h, s0)
            e.activation(out=smdum[:, 0:1], in_=sm[:, 63:64], func=AF.Identity,
                         bias=0.0, scale=1.0).then_inc(scc.h, 1)
        P.on("scalar", fsqrt)
        P.wait("vector", scc, s1)
        v4 = vc.take()
        P.on("vector", lambda e: e.reciprocal(
            out=sm[:, 48:64], in_=sm[:, 48:64]).then_inc(vc.h, 1))
        P.wait("vector", vc, v4)

        tdone = [0] * NT
        edone = [0] * NT
        for tt in range(NT):
            xt = x_sb[:, tt * 1024:(tt + 1) * 1024]
            ub = stg[:, (tt % 2) * 1024:(tt % 2 + 1) * 1024]
            if tt >= 2:
                P.wait("vector", tc, tdone[tt - 2])
            va = vc.take()

            def fapply(e, xt=xt, ub=ub, tt=tt):
                e.tensor_scalar(out=ub, in0=xt, scalar1=sm[:, 32 + tt:33 + tt],
                                scalar2=sm[:, 48 + tt:49 + tt], op0=ALU.subtract,
                                op1=ALU.mult).then_inc(vc.h, 1)
            P.on("vector", fapply)
            ptb = pB[:, (tt % 2) * 512:(tt % 2 + 1) * 512].bitcast(BF16)
            P.wait("tensor", vc, va)
            if tt >= 2:
                P.wait("tensor", scc, edone[tt - 2])
            prevt = tc.n
            td = tc.take(auto=False)

            def ftr(e, ub=ub, ptb=ptb, prevt=prevt):
                for dk in range(NDK):
                    if dk == NDK - 1 and prevt > 0:
                        e.wait_ge(tc.h, prevt)
                    ins = e.transpose(out=ptb[:, dk * 128:(dk + 1) * 128],
                                      in_=ub[:, dk * 128:(dk + 1) * 128],
                                      identity=id_bf[:, :])
                ins.then_inc(tc.h, 1)
            P.on("tensor", ftr)
            tdone[tt] = td
            P.wait("scalar", tc, td)
            pv0 = scc.take()
            sd = scc.take()

            def fev(e, ptb=ptb, tt=tt, pv0=pv0):
                d3 = uT[:, :].rearrange("p (dk t) -> p dk t", t=2048)[:, :, tt * 128:(tt + 1) * 128]
                e.copy(out=d3,
                       in_=ptb.rearrange("p (dk t) -> p dk t", t=128)).then_inc(scc.h, 1)
                e.wait_ge(scc.h, pv0)
                e.activation(out=smdum[:, 0:1], in_=uT[:, tt * 128:tt * 128 + 1],
                             func=AF.Identity, bias=0.0, scale=1.0).then_inc(scc.h, 1)
            P.on("scalar", fev)
            edone[tt] = sd

    def emit_router(level=3):
        if "dcstg_all" in HMS:
            P.wait("tensor", dcstg, HMS["dcstg_all"])
            P.wait("tensor", dcst, HMS["dcst_all"])
            P.wait("scalar", dcstg, HMS["dcstg_all"])
            P.wait("vector", dcstg, HMS["dcstg_all"])
        exp_done = [0] * NT
        tr_ev = [0] * NT
        for tt in range(NT):
            par = tt % 2
            pl = pF[:, par * 512:par * 512 + 64]
            if tt >= 2:
                P.wait("tensor", scc, exp_done[tt - 2])
            prevt = tc.n
            td = tc.take(auto=False)

            def fmm(e, pl=pl, tt=tt, prevt=prevt):
                for dk in range(NDK):
                    e.matmul(out=pl,
                             lhsT=uT[:, dk * 2048 + tt * 128:dk * 2048 + (tt + 1) * 128],
                             rhs=wr_sb[:, dk * 64:(dk + 1) * 64], start=(dk == 0),
                             stop=False, skip_group_check=True)
                if prevt > 0:
                    e.wait_ge(tc.h, prevt)
                e.matmul(out=pl, lhsT=ones1[:, :], rhs=rb_sb[:, :], start=False,
                         stop=True, skip_group_check=True).then_inc(tc.h, 1)
            P.on("tensor", fmm)
            P.wait("vector", tc, td)
            if tt >= 2:
                P.wait("vector", scc, exp_done[tt - 2])
            vm = vc.take()
            P.on("vector", lambda e, pl=pl, par=par: e.reduce_max(
                out=sm[:, par:par + 1], in_=pl, axis=AX.X,
                negate=True).then_inc(vc.h, 1))
            P.wait("scalar", vc, vm)
            pv0 = scc.take()
            sd = scc.take()

            def fexp(e, pl=pl, tt=tt, par=par, pv0=pv0):
                e.activation(out=scr[:, tt * 64:(tt + 1) * 64], in_=pl, func=AF.Exp,
                             bias=sm[:, par:par + 1], scale=1.0).then_inc(scc.h, 1)
                e.wait_ge(scc.h, pv0)
                e.activation(out=smdum[:, 0:1], in_=scr[:, tt * 64 + 63:tt * 64 + 64],
                             func=AF.Identity, bias=0.0, scale=1.0).then_inc(scc.h, 1)
            P.on("scalar", fexp)
            exp_done[tt] = sd

        E3 = scr[:, 0:1024].rearrange("p (t v) -> p t v", v=64)
        C = scr[:, 1024:2048]
        C3 = C.rearrange("p (t v) -> p t v", v=64)
        M = hs_tmp[:, 0:1024]
        M3 = M.rearrange("p (t v) -> p t v", v=64)
        mx = sm[:, 16:32]
        mxb = mx.unsqueeze(2).broadcast_to((128, 16, 64))
        P.wait("vector", scc)
        vals = [vc.take() for _ in range(3 * (TOPK - 1) + 8)]

        def ftopk(e, vals=tuple(vals)):
            i = [0]

            def step(ins):
                ins.then_inc(vc.h, 1)
                e.wait_ge(vc.h, vals[i[0]])
                i[0] += 1
            step(e.tensor_copy(out=C, in_=scr[:, 0:1024]))
            for _ in range(TOPK - 1):
                step(e.tensor_reduce(out=mx, in_=C3, axis=AX.X, op=ALU.max))
                step(e.tensor_tensor(out=M3, in0=C3, in1=mxb, op=ALU.is_ge))
                step(e.scalar_tensor_tensor(out=C, in0=M, scalar=-1e30, in1=C,
                                            op0=ALU.mult, op1=ALU.add))
            step(e.tensor_reduce(out=mx, in_=C3, axis=AX.X, op=ALU.max))
            step(e.tensor_tensor(out=M3, in0=E3, in1=mxb, op=ALU.is_ge))
            step(e.tensor_tensor(out=C, in0=M, in1=scr[:, 0:1024], op=ALU.mult))
            step(e.tensor_reduce(out=mx, in_=C3, axis=AX.X, op=ALU.add))
            step(e.reciprocal(out=mx, in_=mx))
            step(e.tensor_tensor(out=cw05[:, :].rearrange("p (t v) -> p t v", v=64),
                                 in0=C3, in1=mxb, op=ALU.mult))
            step(e.tensor_copy(out=smdum[:, 1:2], in_=smdum[:, 2:3]))
            assert i[0] == len(vals)
        P.on("vector", ftopk)
        cw_done = vals[-1]

        trc_done = [0] * NT
        for tt in range(NT):
            par = tt % 2
            cw16 = stg[:, par * 64:par * 64 + 64].bitcast(F16)
            P.wait("vector", vc, cw_done)
            if tt >= 2:
                P.wait("vector", tc, trc_done[tt - 2])
            vcast = vc.take()

            def fcwcast(e, tt=tt, cw16=cw16):
                e.tensor_copy(out=cw16,
                              in_=cw05[:, tt * 64:(tt + 1) * 64]).then_inc(vc.h, 1)
            P.on("vector", fcwcast)
            ptr16 = pF[:, 1024 + par * 512:1024 + par * 512 + 64].bitcast(F16)
            P.wait("tensor", vc, vcast)
            if tt >= 2:
                P.wait("tensor", scc, tr_ev[tt - 2])
            prevt = tc.n
            td2 = tc.take(auto=False)

            def ftrc(e, ptr16=ptr16, cw16=cw16, prevt=prevt):
                if prevt > 0:
                    e.wait_ge(tc.h, prevt)
                e.transpose(out=ptr16[0:64, :], in_=cw16,
                            identity=id_f16[:, :]).then_inc(tc.h, 1)
            P.on("tensor", ftrc)
            trc_done[tt] = td2
            P.wait("scalar", tc, td2)
            pv0 = scc.take()
            se = scc.take()

            def fcwev(e, ptr16=ptr16, tt=tt, pv0=pv0):
                e.copy(out=cwT[0:64, tt * 128:(tt + 1) * 128],
                       in_=ptr16[0:64, :]).then_inc(scc.h, 1)
                e.wait_ge(scc.h, pv0)
                e.activation(out=smdum[0:64, 0:1], in_=cwT[0:64, tt * 128:tt * 128 + 1],
                             func=AF.Identity, bias=0.0, scale=1.0).then_inc(scc.h, 1)
            P.on("scalar", fcwev)
            tr_ev[tt] = se
        HMS["exp_done"] = list(exp_done)
        HMS["tr_ev"] = list(tr_ev)

    def emit_routed(mlevel=4):
        rt_exp = HMS.get("exp_done", [0] * NT)
        rt_tr = HMS.get("tr_ev", [0] * NT)
        slot_free_w1 = [0] * RING
        slot_free_w2 = [0] * RING
        slot_loaded = [None] * RING
        hsw_mm2_done = [0, 0]
        silu_free = [0, 0]
        mm1_bank_free = [0, 0]
        cw_bank_free = [0, 0]
        f_bank_free = [0, 0, 0, 0]

        def load_expert(eidx):
            s = eidx % RING
            need1 = slot_free_w1[s]
            need2 = slot_free_w2[s]
            p1 = dring[s].n
            n1 = dring[s].take(16)
            p2 = dring2[s].n
            n2 = dring2[s].take(16)

            def f(e, eidx=eidx, s=s, need1=need1, need2=need2, p1=p1, p2=p2):
                if need1 > 0:
                    e.wait_ge(tc.h, need1)
                if p1 > 0:
                    e.wait_ge(dring[s].h, p1)
                src1 = w1[eidx:eidx + 1].rearrange("e2 (dk p) h -> p (e2 dk) h", p=128)
                e.dma_start(out=av(s * SLOT, 2048).rearrange("p (dk h) -> p dk h", h=H),
                            in_=src1).then_inc(dring[s].h, 16)
                if need2 > 0:
                    e.wait_ge(tc.h, need2)
                if p2 > 0:
                    e.wait_ge(dring2[s].h, p2)
                src2 = w2[eidx:eidx + 1].rearrange("e2 (hk p) d -> p (e2 hk) d", p=128)
                e.dma_start(out=av(s * SLOT + 2048, 2048).rearrange("p (hk d) -> p hk d", d=1024),
                            in_=src2).then_inc(dring2[s].h, 16)
            P.on("gpsimd", f)
            slot_loaded[s] = (dring[s], n1, dring2[s], n2)

        for eidx in range(RING):
            load_expert(eidx)

        el_mult_done = [0] * G
        f_bank_scc_gate = [rt_exp[14], rt_exp[15],
                           rt_tr[14] if rt_tr[14] else 0,
                           rt_tr[15] if rt_tr[15] else 0]
        for g in range(E // G):
            for c in range(NCH):
                pg = c % 2
                if c == 0:
                    grp_w2 = []
                    pend_loads = []
                for el in range(G):
                    eidx = g * G + el
                    s = eidx % RING
                    prevt = tc.n
                    t_cw = tc.take(auto=False)
                    t_h = [tc.take(auto=False), tc.take(auto=False)]

                    def fexp_mm(e, eidx=eidx, s=s, c=c, pg=pg, first=(c == 0),
                                wh=slot_loaded[s], mb=tuple(mm1_bank_free),
                                need_cwb=cw_bank_free[pg], prevt=prevt, t_cw=t_cw,
                                t_h0=t_h[0], cw_gate=(rt_tr[c * 4 + 3] if g == 0 else 0)):
                        if first:
                            e.wait_ge(wh[0].h, wh[1])
                        if cw_gate > 0:
                            e.wait_ge(scc.h, cw_gate)
                        if need_cwb > 0:
                            e.wait_ge(vc.h, need_cwb)
                        if prevt > 0:
                            e.wait_ge(tc.h, prevt)
                        lhsT = id_bf[:, eidx:eidx + 1].broadcast_to((128, 128))
                        e.matmul(out=pB[:, pg * 512:(pg + 1) * 512], lhsT=lhsT,
                                 rhs=cwT[:, c * 512:(c + 1) * 512], start=True,
                                 stop=True, skip_group_check=True).then_inc(tc.h, 1)
                        for hk in range(2):
                            if mb[hk] > 0:
                                e.wait_ge(scc.h, mb[hk])
                            for dk in range(NDK):
                                if dk == NDK - 1:
                                    e.wait_ge(tc.h, t_cw if hk == 0 else t_h0)
                                ins = e.matmul(
                                    out=pA[:, hk * 512:(hk + 1) * 512],
                                    lhsT=av(s * SLOT + dk * 256 + hk * 128, 128),
                                    rhs=uT[:, dk * 2048 + c * 512:dk * 2048 + (c + 1) * 512],
                                    start=(dk == 0), stop=(dk == NDK - 1),
                                    skip_group_check=True)
                            ins.then_inc(tc.h, 1)
                    P.on("tensor", fexp_mm)
                    if c == 0:
                        grp_w2.append((slot_loaded[s][2], slot_loaded[s][3]))
                    if c == NCH - 1:
                        slot_free_w1[s] = t_h[1]
                        if eidx + RING < E:
                            pend_loads.append(eidx + RING)
                    s_h = []
                    for hk in range(2):
                        pv0 = scc.take()
                        sv = scc.take()
                        s_h.append(sv)

                        def fsilu(e, hk=hk, eidx=eidx, need_t=t_h[hk],
                                  need_v=silu_free[hk], pv0=pv0):
                            e.wait_ge(tc.h, need_t)
                            if need_v > 0:
                                e.wait_ge(vc.h, need_v)
                            e.activation(out=hs_tmp[:, hk * 512:(hk + 1) * 512],
                                         in_=pA[:, hk * 512:(hk + 1) * 512],
                                         func=AF.Silu,
                                         bias=b1T_sb[:, hk * 64 + eidx:hk * 64 + eidx + 1],
                                         scale=1.0).then_inc(scc.h, 1)
                            e.wait_ge(scc.h, pv0)
                            e.activation(out=smdum[:, 0:1],
                                         in_=hs_tmp[:, hk * 512:hk * 512 + 1],
                                         func=AF.Identity, bias=0.0,
                                         scale=1.0).then_inc(scc.h, 1)
                        P.on("scalar", fsilu)
                        mm1_bank_free[hk] = pv0
                    for hk in range(2):
                        vm = vc.take()
                        hsw_dst = av(HSW_OFF + pg * 4096 + el * 1024 + hk * 512, 512)

                        def fmul(e, hk=hk, hsw_dst=hsw_dst, pg=pg, need_s=s_h[hk],
                                 need_t=t_cw, need_reuse=hsw_mm2_done[pg]):
                            e.wait_ge(scc.h, need_s)
                            e.wait_ge(tc.h, need_t)
                            if need_reuse > 0 and hk == 0:
                                e.wait_ge(tc.h, need_reuse)
                            e.tensor_tensor(out=hsw_dst,
                                            in0=hs_tmp[:, hk * 512:(hk + 1) * 512],
                                            in1=pB[:, pg * 512:(pg + 1) * 512],
                                            op=ALU.mult).then_inc(vc.h, 1)
                        P.on("vector", fmul)
                        silu_free[hk] = vm
                    cw_bank_free[pg] = vm
                    el_mult_done[el] = vm
                for half in range(2):
                    for ttp in range(2):
                        r0 = ttp * 2
                        tt = c * 4 + half * 2 + ttp
                        pf2 = pF[:, r0 * 512:(r0 + 2) * 512]
                        prevt2 = tc.n
                        tm = tc.take(auto=False)

                        def fmm2(e, g=g, pg=pg, half=half, ttp=ttp, pf2=pf2,
                                 needs=tuple(el_mult_done),
                                 nf=(f_bank_free[r0], f_bank_free[r0 + 1]),
                                 prevt2=prevt2,
                                 sg=(f_bank_scc_gate[r0], f_bank_scc_gate[r0 + 1]),
                                 w2h=(tuple(grp_w2) if (c == 0 and half == 0
                                      and ttp == 0) else ())):
                            for hh_, nn_ in w2h:
                                e.wait_ge(hh_.h, nn_)
                            for g_ in sg:
                                if g_ > 0:
                                    e.wait_ge(scc.h, g_)
                            if max(nf) > 0:
                                e.wait_ge(vc.h, max(nf))
                            i = 0
                            for el in range(G):
                                e.wait_ge(vc.h, needs[el])
                                for hk in range(2):
                                    for dc in range(2):
                                        if (i == 2 * G - 1 and dc == 1
                                                and prevt2 > 0):
                                            e.wait_ge(tc.h, prevt2)
                                        ins = e.matmul(
                                            out=pf2[:, dc * 512:(dc + 1) * 512],
                                            lhsT=av(HSW_OFF + pg * 4096 + el * 1024
                                                    + hk * 512 + (half * 2 + ttp) * 128, 128),
                                            rhs=av(((g * G + el) % RING) * SLOT + 2048
                                                   + hk * 1024 + dc * 512, 512),
                                            start=(i == 0), stop=(i == 2 * G - 1),
                                            skip_group_check=True)
                                    i += 1
                            ins.then_inc(tc.h, 1)
                        P.on("tensor", fmm2)
                        f_bank_scc_gate[r0] = 0
                        f_bank_scc_gate[r0 + 1] = 0
                        ve = vc.take()

                        def fev(e, pf2=pf2, tt=tt, need=tm):
                            e.wait_ge(tc.h, need)
                            xt = x_sb[:, tt * 1024:(tt + 1) * 1024]
                            e.scalar_tensor_tensor(out=xt, in0=pf2, scalar=0.5, in1=xt,
                                                   op0=ALU.mult,
                                                   op1=ALU.add).then_inc(vc.h, 1)
                        P.on("vector", fev)
                        f_bank_free[r0] = ve
                        f_bank_free[r0 + 1] = ve
                hsw_mm2_done[pg] = tc.n
                if c == NCH - 1:
                    for el in range(G):
                        slot_free_w2[(g * G + el) % RING] = tc.n
                    for nxt in pend_loads:
                        load_expert(nxt)

    def emit_shared():
        f_bank_free = [0, 0, 0, 0]
        hs_free = 0
        sw2_free = 0
        sw1_free = [0, 0]
        for es in range(ES):
            for hh in range(2):
                idx = es * 2 + hh
                ping = idx % 2
                n1 = dsw[ping].take(16)
                n2 = dsw[2].take(16)

                pp1 = dsw[ping].n - 16
                pp2 = dsw[2].n - 16

                def fload(e, es=es, hh=hh, ping=ping, need1=sw1_free[ping],
                          need2=sw2_free, pp1=pp1, pp2=pp2):
                    if need1 > 0:
                        e.wait_ge(tc.h, need1)
                    if pp1 > 0:
                        e.wait_ge(dsw[ping].h, pp1)
                    src1 = sw1[es:es + 1, :, hh * 1024:(hh + 1) * 1024].rearrange(
                        "e2 (dk p) h -> p (e2 dk) h", p=128)
                    e.dma_start(out=av(ping * 8192, 8192).rearrange(
                        "p (dk h) -> p dk h", h=1024), in_=src1).then_inc(dsw[ping].h, 16)
                    if need2 > 0:
                        e.wait_ge(tc.h, need2)
                    if pp2 > 0:
                        e.wait_ge(dsw[2].h, pp2)
                    src2 = sw2[es, hh * 1024:(hh + 1) * 1024, :].rearrange(
                        "(hk p) d -> p hk d", p=128)
                    e.dma_start(out=av(16384, 8192).rearrange(
                        "p (hk d) -> p hk d", d=1024), in_=src2).then_inc(dsw[2].h, 16)
                P.on("gpsimd", fload)
                mm1_bank_free = [0, 0]
                for th in range(2):
                    hs_gate = hs_free
                    for hkl in range(8):
                        for cc in range(2):
                            pg = (hkl * 2 + cc) % 2
                            prevt = tc.n
                            tm = tc.take(auto=False)

                            def fmm1(e, hkl=hkl, cc=cc, th=th, pg=pg, ping=ping,
                                     first=(th == 0 and hkl == 0 and cc == 0), wl=n1,
                                     need_b=mm1_bank_free[pg], prevt=prevt):
                                if first:
                                    e.wait_ge(dsw[ping].h, wl)
                                if need_b > 0:
                                    e.wait_ge(scc.h, need_b)
                                for dk in range(NDK):
                                    if dk == NDK - 1 and prevt > 0:
                                        e.wait_ge(tc.h, prevt)
                                    ins = e.matmul(
                                        out=pA[:, pg * 512:(pg + 1) * 512],
                                        lhsT=av(ping * 8192 + dk * 1024 + hkl * 128, 128),
                                        rhs=uT[:, dk * 2048 + th * 1024 + cc * 512:
                                               dk * 2048 + th * 1024 + (cc + 1) * 512],
                                        start=(dk == 0), stop=(dk == NDK - 1),
                                        skip_group_check=True)
                                ins.then_inc(tc.h, 1)
                            P.on("tensor", fmm1)
                            pv0 = scc.take()
                            sv = scc.take()

                            def fsilu(e, hkl=hkl, cc=cc, hh=hh, es=es, pg=pg, need=tm,
                                      gate=(hs_gate if (hkl == 0 and cc == 0) else 0),
                                      pv0=pv0):
                                e.wait_ge(tc.h, need)
                                if gate > 0:
                                    e.wait_ge(tc.h, gate)
                                hkg = hh * 8 + hkl
                                e.activation(
                                    out=av(24576 + hkl * 1024 + cc * 512, 512),
                                    in_=pA[:, pg * 512:(pg + 1) * 512], func=AF.Silu,
                                    bias=sb1T_sb[:, hkg * 2 + es:hkg * 2 + es + 1],
                                    scale=1.0).then_inc(scc.h, 1)
                                e.wait_ge(scc.h, pv0)
                                e.activation(out=smdum[:, 0:1],
                                             in_=av(24576 + hkl * 1024 + cc * 512, 1),
                                             func=AF.Identity, bias=0.0,
                                             scale=1.0).then_inc(scc.h, 1)
                            P.on("scalar", fsilu)
                            mm1_bank_free[pg] = pv0
                    silu_all = scc.n
                    last_terms = (es == ES - 1 and hh == 1)
                    for ttl in range(8):
                        tt = th * 8 + ttl
                        for dc in range(2):
                            r = (ttl * 2 + dc) % 4
                            pf = pF[:, r * 512:(r + 1) * 512]
                            prevt = tc.n
                            tm = tc.take(auto=False)

                            def fmm2(e, tt=tt, ttl=ttl, dc=dc, pf=pf, need_s=silu_all,
                                     need_f=f_bank_free[r], wl=n2,
                                     first=(ttl == 0 and dc == 0), lt=last_terms,
                                     prevt=prevt):
                                e.wait_ge(scc.h, need_s)
                                if first:
                                    e.wait_ge(dsw[2].h, wl)
                                if need_f > 0:
                                    e.wait_ge(vc.h, need_f)
                                for hkl in range(8):
                                    if hkl == 7 and not lt and prevt > 0:
                                        e.wait_ge(tc.h, prevt)
                                    ins = e.matmul(
                                        out=pf,
                                        lhsT=av(24576 + hkl * 1024 + ttl * 128, 128),
                                        rhs=av(16384 + hkl * 1024 + dc * 512, 512),
                                        start=(hkl == 0), stop=(hkl == 7 and not lt),
                                        skip_group_check=True)
                                if lt:
                                    if prevt > 0:
                                        e.wait_ge(tc.h, prevt)
                                    ins = e.matmul(out=pf, lhsT=cwT[:, tt * 128:(tt + 1) * 128],
                                                   rhs=b2_sb[:, dc * 512:(dc + 1) * 512],
                                                   start=False, stop=True,
                                                   skip_group_check=True)
                                ins.then_inc(tc.h, 1)
                            P.on("tensor", fmm2)
                            ve = vc.take()

                            def fev(e, pf=pf, tt=tt, dc=dc, need=tm):
                                e.wait_ge(tc.h, need)
                                xt = x_sb[:, tt * 1024 + dc * 512:tt * 1024 + (dc + 1) * 512]
                                e.scalar_tensor_tensor(out=xt, in0=pf, scalar=0.5, in1=xt,
                                                       op0=ALU.mult,
                                                       op1=ALU.add).then_inc(vc.h, 1)
                            P.on("vector", fev)
                            f_bank_free[r] = ve
                    hs_free = tc.n
                sw1_free[ping] = tc.n
                sw2_free = tc.n

    def emit_attention():
        KN = 16384
        WA, WB = 0, 8192

        def load_w(off, src, gate=0):
            pp = dsw[3].n
            n = dsw[3].take(16)

            def f(e, off=off, src=src, gate=gate, pp=pp):
                if gate > 0:
                    e.wait_ge(tc.h, gate)
                if pp > 0:
                    e.wait_ge(dsw[3].h, pp)
                e.dma_start(out=av(off, 8192).rearrange("p (dk n2) -> p dk n2", n2=1024),
                            in_=src.rearrange("(dk p) n2 -> p dk n2", p=128)).then_inc(dsw[3].h, 16)
            P.on("gpsimd", f)
            return n

        wk_h = load_w(WA, w_k)
        wv_h = load_w(WB, w_v)
        kland_pf = pF[:, 0:512]

        def PQ(par):
            return pA if par == 0 else pB

        def softmax_chain(tt, pre_done, out_f16, accq, norm_gate=0):
            par = tt % 2
            P.wait("vector", tc, pre_done)
            if accq[0] > 0:
                P.wait("vector", scc, accq[0])
            vm = vc.take()
            P.on("vector", lambda e, par=par: e.reduce_max(
                out=sm[:, par:par + 1], in_=PQ(par)[:, :], axis=AX.X,
                negate=True).then_inc(vc.h, 1))
            P.wait("scalar", vc, vm)
            if accq[1] > 0:
                P.wait("scalar", vc, accq[1])
            pv0 = scc.take()
            sd = scc.take()

            def fexp(e, par=par, pv0=pv0):
                e.activation(out=scr[:, par * 1024:(par + 1) * 1024], in_=PQ(par)[:, :],
                             func=AF.Exp, bias=sm[:, par:par + 1], scale=1.0,
                             accum_out=sm[:, 2 + par:3 + par]).then_inc(scc.h, 1)
                e.wait_ge(scc.h, pv0)
                e.activation(out=smdum[:, 0:1], in_=sm[:, 2 + par:3 + par],
                             func=AF.Identity, bias=0.0, scale=1.0).then_inc(scc.h, 1)
            P.on("scalar", fexp)
            accq[0] = sd
            P.wait("vector", scc, sd)
            v1 = vc.take()
            P.on("vector", lambda e, par=par: e.reciprocal(
                out=sm[:, 4 + par:5 + par], in_=sm[:, 2 + par:3 + par]).then_inc(vc.h, 1))
            P.wait("vector", vc, v1)
            if norm_gate > 0:
                P.wait("vector", tc, norm_gate)
            v2 = vc.take()
            P.on("vector", lambda e, par=par, out_f16=out_f16: e.tensor_scalar(
                out=out_f16, in0=scr[:, par * 1024:(par + 1) * 1024],
                scalar1=sm[:, 4 + par:5 + par], scalar2=None,
                op0=ALU.mult).then_inc(vc.h, 1))
            accq[1] = v2
            return sd, v2

        def pre_mm(tt, woff, bias_sb, gate_scc, gate_vc, wh):
            par = tt % 2
            prevt = tc.n
            tp = tc.take(auto=False)

            def fpre(e, tt=tt, par=par, woff=woff, bias_sb=bias_sb, wh=wh,
                     gs=gate_scc, gv=gate_vc, prevt=prevt):
                if wh is not None:
                    e.wait_ge(dsw[3].h, wh)
                if gs > 0:
                    e.wait_ge(scc.h, gs)
                if gv > 0:
                    e.wait_ge(vc.h, gv)
                for dc in range(2):
                    for dk in range(NDK):
                        e.matmul(out=PQ(par)[:, dc * 512:(dc + 1) * 512],
                                 lhsT=uT[:, dk * 2048 + tt * 128:dk * 2048 + (tt + 1) * 128],
                                 rhs=av(woff + dk * 1024 + dc * 512, 512),
                                 start=(dk == 0), stop=False, skip_group_check=True)
                    if dc == 1 and prevt > 0:
                        e.wait_ge(tc.h, prevt)
                    ins = e.matmul(out=PQ(par)[:, dc * 512:(dc + 1) * 512],
                                   lhsT=ones1[:, :],
                                   rhs=bias_sb[:, dc * 512:(dc + 1) * 512], start=False,
                                   stop=True, skip_group_check=True)
                ins.then_inc(tc.h, 1)
            P.on("tensor", fpre)
            return tp

        exp_par = [[0, 0], [0, 0]]
        norm_done = [0] * NT
        kland_done = 0
        pre_done = [0] * NT
        for tt in range(NT):
            par = tt % 2
            gate = exp_par[par][0]
            pre_done[tt] = pre_mm(tt, WA, bk_sb, gate, 0, wk_h if tt == 0 else None)
            kn = av16(KN + tt * 1024, 1024)
            sd, v2 = softmax_chain(tt, pre_done[tt], kn, exp_par[par])
            norm_done[tt] = v2
            prevt = tc.n
            td = tc.take(auto=False)

            def fkl(e, tt=tt, kn=kn, need=v2, prevt=prevt):
                e.wait_ge(vc.h, need)
                for dk in range(NDK):
                    if dk == NDK - 1 and prevt > 0:
                        e.wait_ge(tc.h, prevt)
                    ins = e.matmul(out=kland_pf[:, dk * 64:(dk + 1) * 64],
                                   lhsT=kn[:, dk * 128:(dk + 1) * 128],
                                   rhs=apat[:, tt * 64:(tt + 1) * 64],
                                   start=(tt == 0 and dk == 0),
                                   stop=(tt == NT - 1 and dk == NDK - 1),
                                   skip_group_check=True)
                ins.then_inc(tc.h, 1)
            P.on("tensor", fkl)
            kland_done = td
        P.wait("scalar", tc, kland_done)
        kl0 = scc.take()
        klT_d = scc.take()

        def fklev(e, kl0=kl0):
            e.copy(out=klT[:, :], in_=kland_pf).then_inc(scc.h, 1)
            e.wait_ge(scc.h, kl0)
            e.activation(out=smdum[:, 0:1], in_=klT[:, 0:1], func=AF.Identity,
                         bias=0.0, scale=1.0).then_inc(scc.h, 1)
        P.on("scalar", fklev)

        wq_h = load_w(WA, w_q, gate=kland_done)

        mid_pf0 = pF[0:64, 0:512]
        mid_pf1 = pF[0:64, 512:1024]
        exp_par = [[0, 0], [0, 0]]
        mid_tc = [0] * NT
        k3_done = [0] * NT
        kts_done = [0] * NT
        k2s_done = [0] * NT
        tr_bank_free = [0, 0]
        k2_bank_free = [0, 0]
        mid_done = 0

        def a2_tail(tt):
            nonlocal mid_done
            par = tt % 2
            kn = av16(KN + tt * 1024, 1024)
            ptb = pF[:, 1024 + par * 512:1024 + (par + 1) * 512].bitcast(F16)
            prevt = tc.n
            td = tc.take(auto=False)

            def ftr(e, kn=kn, ptb=ptb, prevt=prevt, g1=tr_bank_free[par],
                    g2=k2_bank_free[par]):
                if g1 > 0:
                    e.wait_ge(scc.h, g1)
                if g2 > 0:
                    e.wait_ge(scc.h, g2)
                for dk in range(NDK):
                    if dk == NDK - 1 and prevt > 0:
                        e.wait_ge(tc.h, prevt)
                    ins = e.transpose(out=ptb[:, dk * 128:(dk + 1) * 128],
                                      in_=kn[:, dk * 128:(dk + 1) * 128],
                                      identity=id_f16[:, :])
                ins.then_inc(tc.h, 1)
            P.on("tensor", ftr)
            P.wait("scalar", tc, td)
            kt0 = scc.take()
            ktsd = scc.take()
            kts = stg[:, 2048 + par * 1024:2048 + (par + 1) * 1024].bitcast(F16)

            def fktsev(e, ptb=ptb, kts=kts, kt0=kt0):
                e.copy(out=kts, in_=ptb).then_inc(scc.h, 1)
                e.wait_ge(scc.h, kt0)
                e.activation(out=smdum[:, 0:1], in_=kts[:, 0:1],
                             func=AF.Identity, bias=0.0, scale=1.0).then_inc(scc.h, 1)
            P.on("scalar", fktsev)
            kts_done[tt] = ktsd
            tr_bank_free[par] = ktsd
            k2o = pF[:, 1024 + par * 512:1024 + par * 512 + 64]
            prevt = tc.n
            tk2 = tc.take(auto=False)

            def fk2(e, kts=kts, k2o=k2o, need_s=ktsd, need_kl=klT_d, prevt=prevt):
                e.wait_ge(scc.h, max(need_s, need_kl))
                for dk in range(NDK):
                    if dk == NDK - 1 and prevt > 0:
                        e.wait_ge(tc.h, prevt)
                    ins = e.matmul(out=k2o, lhsT=kts[:, dk * 128:(dk + 1) * 128],
                                   rhs=klT[:, dk * 64:(dk + 1) * 64], start=(dk == 0),
                                   stop=(dk == NDK - 1), skip_group_check=True)
                ins.then_inc(tc.h, 1)
            P.on("tensor", fk2)
            P.wait("scalar", tc, tk2)
            k20 = scc.take()
            k2sd = scc.take()

            def fk2ev(e, k2o=k2o, par=par, k20=k20):
                e.copy(out=k2s[:, par * 64:(par + 1) * 64], in_=k2o).then_inc(scc.h, 1)
                e.wait_ge(scc.h, k20)
                e.activation(out=smdum[:, 0:1], in_=k2s[:, par * 64:par * 64 + 1],
                             func=AF.Identity, bias=0.0, scale=1.0).then_inc(scc.h, 1)
            P.on("scalar", fk2ev)
            k2s_done[tt] = k2sd
            k2_bank_free[par] = k2sd
            k3 = stg[:, par * 1024:(par + 1) * 1024].bitcast(F16)
            prevt = tc.n
            tmid = tc.take(auto=False)

            def fmid(e, k3=k3, tt=tt, par=par, need_s=k2sd, need_v=k3_done[tt],
                     need_kl=klT_d, prevt=prevt):
                e.wait_ge(scc.h, max(need_s, need_kl))
                e.wait_ge(vc.h, need_v)
                for dc, pfm in ((0, mid_pf0), (1, mid_pf1)):
                    if dc == 1 and prevt > 0:
                        e.wait_ge(tc.h, prevt)
                    ins = e.matmul(out=pfm, lhsT=k2s[:, par * 64:(par + 1) * 64],
                                   rhs=k3[:, dc * 512:(dc + 1) * 512],
                                   start=(tt == 0), stop=(tt == NT - 1),
                                   skip_group_check=True)
                ins.then_inc(tc.h, 1)
            P.on("tensor", fmid)
            mid_tc[tt] = tmid
            mid_done = tmid

        exp_parA = exp_par
        for tt in range(NT):
            par = tt % 2
            gate_v = k3_done[tt - 2] if tt >= 2 else 0
            pre_done[tt] = pre_mm(tt, WB, bv_sb, exp_parA[par][0], gate_v,
                                  wv_h if tt == 0 else None)
            exp_parA[par][0] = 0
            P.wait("vector", tc, pre_done[tt])
            if tt >= 2:
                P.wait("vector", tc, mid_tc[tt - 2])
            kn = av16(KN + tt * 1024, 1024)
            k3 = stg[:, par * 1024:(par + 1) * 1024].bitcast(F16)
            v3 = vc.take()
            P.on("vector", lambda e, kn=kn, k3=k3, par=par: e.tensor_tensor(
                out=k3, in0=kn, in1=PQ(par)[:, :], op=ALU.mult).then_inc(vc.h, 1))
            k3_done[tt] = v3
            if tt >= 1:
                a2_tail(tt - 1)
        a2_tail(NT - 1)
        P.wait("scalar", tc, mid_done)
        md0 = scc.take()
        md1 = scc.take()
        mid_d = scc.take()

        def fmidev(e, md0=md0, md1=md1):
            e.copy(out=midb[:, 0:512], in_=mid_pf0).then_inc(scc.h, 1)
            e.wait_ge(scc.h, md0)
            e.copy(out=midb[:, 512:1024], in_=mid_pf1).then_inc(scc.h, 1)
            e.wait_ge(scc.h, md1)
            e.activation(out=smdum[0:64, 0:1], in_=midb[:, 0:1],
                         func=AF.Identity, bias=0.0, scale=1.0).then_inc(scc.h, 1)
        P.on("scalar", fmidev)

        exp_par = [[0, 0], [0, 0]]
        qn_done = [0] * NT
        out_free = [0, 0]
        trq_free = [0, 0]
        k1_bank_free = [0, 0]
        trq_done = [0] * NT

        def b_tail(tt):
            par = tt % 2
            qn = stg[:, par * 1024:(par + 1) * 1024].bitcast(F16)
            ptb = pF[:, 1024 + par * 512:1024 + (par + 1) * 512].bitcast(F16)
            prevt = tc.n
            td = tc.take(auto=False)

            def ftrq(e, qn=qn, ptb=ptb, prevt=prevt, need_v=qn_done[tt],
                     g1=trq_free[par], g2=k1_bank_free[par]):
                e.wait_ge(vc.h, need_v)
                if g1 > 0:
                    e.wait_ge(scc.h, g1)
                if g2 > 0:
                    e.wait_ge(scc.h, g2)
                for dk in range(NDK):
                    if dk == NDK - 1 and prevt > 0:
                        e.wait_ge(tc.h, prevt)
                    ins = e.transpose(out=ptb[:, dk * 128:(dk + 1) * 128],
                                      in_=qn[:, dk * 128:(dk + 1) * 128],
                                      identity=id_f16[:, :])
                ins.then_inc(tc.h, 1)
            P.on("tensor", ftrq)
            trq_done[tt] = td
            P.wait("scalar", tc, td)
            qt0 = scc.take()
            qtsd = scc.take()
            qts = stg[:, 2048 + par * 1024:2048 + (par + 1) * 1024].bitcast(F16)

            def fqtsev(e, ptb=ptb, qts=qts, qt0=qt0):
                e.copy(out=qts, in_=ptb).then_inc(scc.h, 1)
                e.wait_ge(scc.h, qt0)
                e.activation(out=smdum[:, 0:1], in_=qts[:, 0:1],
                             func=AF.Identity, bias=0.0, scale=1.0).then_inc(scc.h, 1)
            P.on("scalar", fqtsev)
            trq_free[par] = qtsd
            k1o = pF[0:64, 1024 + par * 512:1024 + par * 512 + 128]
            prevt = tc.n
            tk1 = tc.take(auto=False)

            def fk1(e, qts=qts, k1o=k1o, need=qtsd, need_kl=klT_d, prevt=prevt):
                e.wait_ge(scc.h, max(need, need_kl))
                for dk in range(NDK):
                    if dk == NDK - 1 and prevt > 0:
                        e.wait_ge(tc.h, prevt)
                    ins = e.matmul(out=k1o,
                                   lhsT=klT[:, dk * 64:(dk + 1) * 64],
                                   rhs=qts[:, dk * 128:(dk + 1) * 128], start=(dk == 0),
                                   stop=(dk == NDK - 1), skip_group_check=True)
                ins.then_inc(tc.h, 1)
            P.on("tensor", fk1)
            P.wait("scalar", tc, tk1)
            k10 = scc.take()
            k1sd = scc.take()

            def fk1ev(e, k1o=k1o, par=par, k10=k10):
                e.copy(out=k1s[:, par * 128:(par + 1) * 128], in_=k1o).then_inc(scc.h, 1)
                e.wait_ge(scc.h, k10)
                e.activation(out=smdum[0:64, 0:1], in_=k1s[:, par * 128:par * 128 + 1],
                             func=AF.Identity, bias=0.0, scale=1.0).then_inc(scc.h, 1)
            P.on("scalar", fk1ev)
            k1_bank_free[par] = k1sd
            for dc in range(2):
                po = pF[:, dc * 512:(dc + 1) * 512]
                prevt = tc.n
                to = tc.take(auto=False)

                def fo(e, po=po, par=par, dc=dc, need_s=k1sd, need_m=mid_d,
                       need_f=out_free[dc], prevt=prevt):
                    e.wait_ge(scc.h, max(need_s, need_m))
                    if need_f > 0:
                        e.wait_ge(vc.h, need_f)
                    if prevt > 0:
                        e.wait_ge(tc.h, prevt)
                    e.matmul(out=po, lhsT=k1s[:, par * 128:(par + 1) * 128],
                             rhs=midb[:, dc * 512:(dc + 1) * 512], start=True,
                             stop=True, skip_group_check=True).then_inc(tc.h, 1)
                P.on("tensor", fo)
                vo = vc.take()

                def foev(e, po=po, tt=tt, dc=dc, need=to):
                    e.wait_ge(tc.h, need)
                    xt = x_sb[:, tt * 1024 + dc * 512:tt * 1024 + (dc + 1) * 512]
                    e.tensor_tensor(out=xt, in0=po, in1=xt,
                                    op=ALU.add).then_inc(vc.h, 1)
                P.on("vector", foev)
                out_free[dc] = vo

        for tt in range(NT):
            par = tt % 2
            pq_gate_v = k3_done[14 + par] if tt < 2 else 0
            pre_done[tt] = pre_mm(tt, WA, bq_sb, exp_par[par][0], pq_gate_v,
                                  wq_h if tt == 0 else None)
            qn = stg[:, par * 1024:(par + 1) * 1024].bitcast(F16)
            ng = trq_done[tt - 2] if tt >= 2 else mid_tc[14 + par]
            sd, v2 = softmax_chain(tt, pre_done[tt], qn, exp_par[par], norm_gate=ng)
            qn_done[tt] = v2
            if tt >= 1:
                b_tail(tt - 1)
        b_tail(NT - 1)

    def ph_store():
        P.wait("sync", vc)
        no = dout.take(64)

        def f(e):
            xs = x_out.rearrange("(tt p) d -> p tt d", p=128)
            for i in range(4):
                if i > 0:
                    e.wait_ge(dout.h, 16 * i)
                src = x_sb[:, i * 4096:(i + 1) * 4096].rearrange("p (tt d) -> p tt d", d=1024)
                e.dma_start(out=xs[:, i * 4:(i + 1) * 4, :], in_=src).then_inc(dout.h, 16)
            e.wait_ge(dout.h, no)
        P.on("sync", f)

    emitters = {"ln1": emit_ln, "rt1": emit_router,
                "rt1a": (lambda: emit_router(1)), "rt1b": (lambda: emit_router(2)),
                "m1": (lambda: emit_routed(1)), "m2": (lambda: emit_routed(2)),
                "m3": (lambda: emit_routed(3)),
                "moe1": emit_routed,
                "sh1": emit_shared, "ln2": emit_ln, "attn": emit_attention,
                "ln3": emit_ln, "rt2": emit_router, "moe2": emit_routed,
                "sh2": emit_shared}
    ph_load()
    join()
    for i, ph in enumerate(phases):
        emitters[ph]()
        nxt = phases[i + 1] if i + 1 < len(phases) else None
        if ph.startswith("rt") and nxt is not None and nxt.startswith("moe"):
            continue
        join()
    ph_store()
    if dbg:
        no2 = dout.take(16 * 6)

        def fdbg(e):
            e.wait_ge(dout.h, 64)
            e.dma_start(out=dbg_uT, in_=uT[:, :]).then_inc(dout.h, 16)
            e.wait_ge(dout.h, 80)
            e.dma_start(out=dbg_cw, in_=cw05[:, :]).then_inc(dout.h, 16)
            e.wait_ge(dout.h, 96)
            e.dma_start(out=dbg_cwT, in_=cwT[0:64, :]).then_inc(dout.h, 16)
            e.wait_ge(dout.h, 112)
            e.dma_start(out=dbg_klT, in_=klT[:, :]).then_inc(dout.h, 16)
            e.wait_ge(dout.h, 128)
            e.dma_start(out=dbg_mid, in_=midb[:, :]).then_inc(dout.h, 16)
            e.wait_ge(dout.h, 144)
            e.dma_start(out=dbg_kn, in_=arena[:, 16384:32768].bitcast(F16)).then_inc(dout.h, 16)
            e.wait_ge(dout.h, no2)
        P.on("sync", fdbg)

    with nc.Block() as blk:
        for name in ENGINES:
            def runner(e, fns=tuple(P.q[name])):
                for fn in fns:
                    fn(e)
            getattr(blk, name)(runner)

    stack.close()
    return nc


_NC_CACHE = None


def _consts():
    ident = np.eye(128)
    apat = np.zeros((128, NT * L), np.float16)
    for tt in range(NT):
        for p in range(128):
            li = (tt * 128 + p) // 32
            apat[p, tt * L + li] = 1.0 / 32.0
    return dict(
        c_idbf=ident.astype(ml_dtypes.bfloat16),
        c_idf16=ident.astype(np.float16),
        c_idf32=ident.astype(np.float32),
        c_id64=np.eye(64).astype(np.float16),
        c_ones1=np.ones((1, 128), np.float16),
        c_ones2=np.ones((2, 128), np.float16),
        c_apat=apat,
    )


def kernel(L0_attn_hidden_states, L0_attn_Wq, L0_attn_bq, L0_attn_Wk, L0_attn_bk,
           L0_attn_Wv, L0_attn_bv, L0_ffn_router_w, L0_ffn_router_b, L0_ffn_w1,
           L0_ffn_b1, L0_ffn_w2, L0_ffn_b2, L0_ffn_sw1, L0_ffn_sb1, L0_ffn_sw2,
           L0_ffn_sb2):
    global _NC_CACHE
    if _NC_CACHE is None:
        _NC_CACHE = build_nc()
    nc = _NC_CACHE

    xs = np.ascontiguousarray(np.asarray(L0_attn_hidden_states, dtype=np.float32))
    base = dict(
        wq=np.ascontiguousarray(L0_attn_Wq, np.float32),
        wk=np.ascontiguousarray(L0_attn_Wk, np.float32),
        wv=np.ascontiguousarray(L0_attn_Wv, np.float32),
        bq=np.ascontiguousarray(L0_attn_bq, np.float32).reshape(1, D),
        bk=np.ascontiguousarray(L0_attn_bk, np.float32).reshape(1, D),
        bv=np.ascontiguousarray(L0_attn_bv, np.float32).reshape(1, D),
        router_w=np.ascontiguousarray(L0_ffn_router_w, np.float32),
        router_b=np.ascontiguousarray(L0_ffn_router_b, np.float32).reshape(1, E),
        w1=np.ascontiguousarray(L0_ffn_w1, np.float32),
        b1t=np.ascontiguousarray(np.asarray(L0_ffn_b1, dtype=np.float32).T),
        w2=np.ascontiguousarray(L0_ffn_w2, np.float32),
        b2=np.ascontiguousarray(L0_ffn_b2, np.float32),
        sw1=np.ascontiguousarray(L0_ffn_sw1, np.float32),
        sb1t=np.ascontiguousarray(np.asarray(L0_ffn_sb1, dtype=np.float32).T),
        sw2=np.ascontiguousarray(L0_ffn_sw2, np.float32),
        sb2=np.ascontiguousarray(L0_ffn_sb2, np.float32),
    )
    base.update(_consts())
    in_maps = []
    for b in range(B):
        m = dict(base)
        m["x"] = np.ascontiguousarray(xs[b])
        in_maps.append(m)
    res = run_bass_kernel_spmd(nc, in_maps, core_ids=list(range(B)))
    out = np.stack([np.asarray(res.results[b]["out"]) for b in range(B)], axis=0)
    return out.astype(np.float32)
